# version 20
# baseline (speedup 1.0000x reference)
import contextlib

import numpy as np
import concourse.bass as bass
import concourse.mybir as mybir
from concourse.bass_utils import run_bass_kernel_spmd

F32 = mybir.dt.float32
BF16 = mybir.dt.bfloat16
ALU = mybir.AluOpType
ACT = mybir.ActivationFunctionType

MAGIC = float(np.float32(1.5 * 2.0**23))
D = 2048
KT = 16
DFF = 5504
FT = 43
H = 16
TOK = 1024
NCH = 8
FC = 512
NFC = 2


class P:
    def __init__(self, cur, handle, sems):
        self.cur = cur
        self.h = handle
        self.sems = sems
        self.cnt = {name: 0 for name in sems}

    def on(self, eng):
        return eng == self.cur

    def do(self, eng, fn, sem=None, inc=1):
        if self.on(eng):
            instr = fn(self.h)
            if sem is not None:
                instr.then_inc(self.sems[sem], inc)
        if sem is not None:
            self.cnt[sem] += inc
            return self.cnt[sem]
        return None

    def dma(self, eng, out, in_, sem="d"):
        return self.do(eng, lambda h: h.dma_start(out=out, in_=in_), sem, 16)

    def wait(self, eng, sem, val):
        assert val <= self.cnt[sem], f"future wait {sem}:{val}>{self.cnt[sem]}"
        if self.on(eng):
            self.h.wait_ge(self.sems[sem], val)

    def drain(self, eng):
        if self.on(eng):
            self.h.drain()

    def mm(self, out, lhsT, rhs, start=True, stop=True, sem="t"):
        return self.do("t", lambda h: h.matmul(out, lhsT, rhs, start=start, stop=stop), sem)

    def tp(self, out, in_, ident, sem="t"):
        return self.do("t", lambda h: h.transpose(out, in_, ident), sem)

    def barrier(self, engs="tvsg"):
        for e in engs:
            self.drain(e)
            self.do(e, lambda h: h.sem_inc(self.sems["b" + e], 1))
            self.cnt["b" + e] += 1
        for e in engs:
            for e2 in engs:
                if e2 != e:
                    self.wait(e, "b" + e2, self.cnt["b" + e2])


def build_program(n_cores, stages=("ffn1", "attn", "ffn2"), debug=False):
    nc = bass.Bass("TRN2", target_bir_lowering=False)

    X = nc.dram_tensor("x_fm", [D, TOK], F32, kind="ExternalInput").ap()
    OUT = nc.dram_tensor("out_fm", [D, TOK], F32, kind="ExternalOutput").ap()
    WT = {nm: nc.dram_tensor(nm, [KT, 128, KT, 128], F32, kind="ExternalInput").ap()
          for nm in ("wq", "wk", "wv", "wo")}
    WG = nc.dram_tensor("wg", [D, H], F32, kind="ExternalInput").ap()
    BG = nc.dram_tensor("bg", [H, 1], F32, kind="ExternalInput").ap()
    BO = nc.dram_tensor("bo", [128, KT], F32, kind="ExternalInput").ap()
    QG = KT // n_cores
    QD = -(-FT // n_cores)
    d_counts = [FT // n_cores + (1 if i < FT % n_cores else 0)
                for i in range(n_cores)]
    d_slot = []
    for ci in range(n_cores):
        for l in range(d_counts[ci]):
            d_slot.append(ci * QD + l)
    WFFN = {"g": nc.dram_tensor("wown_g", [QG * 128, DFF], F32,
                                kind="ExternalInput").ap(),
            "u": nc.dram_tensor("wown_u", [QG * 128, DFF], F32,
                                kind="ExternalInput").ap(),
            "d": nc.dram_tensor("wown_d", [QD * 128, D], F32,
                                kind="ExternalInput").ap()}
    TRIGC = nc.dram_tensor("trigc", [128, TOK], F32, kind="ExternalInput").ap()
    TRIGS = nc.dram_tensor("trigs", [128, TOK], F32, kind="ExternalInput").ap()
    MASKT = nc.dram_tensor("maskt", [128, 128], F32, kind="ExternalInput").ap()
    ID128B = nc.dram_tensor("id128b", [128, 128], BF16, kind="ExternalInput").ap()
    ID128F = nc.dram_tensor("id128f", [128, 128], F32, kind="ExternalInput").ap()
    ID16F = nc.dram_tensor("id16f", [16, 16], F32, kind="ExternalInput").ap()
    ONESCOL = nc.dram_tensor("onescol", [128, 1], F32, kind="ExternalInput").ap()
    ONES128 = nc.dram_tensor("ones128", [128, 128], F32, kind="ExternalInput").ap()
    ONESROW = nc.dram_tensor("onesrow", [1, 128], F32, kind="ExternalInput").ap()
    MRECV = nc.dram_tensor("mrecv", [128, 1], F32, kind="ExternalInput").ap()
    CONSTT = nc.dram_tensor("constt", [1, 4], F32, kind="ExternalInput").ap()
    CCQ_IN = {t: nc.dram_tensor("ccq_in_" + t, [q * nt * 128, 128], BF16,
                                kind="Internal").ap()
              for t, q, nt in (("g", QG, FT), ("u", QG, FT), ("d", QD, KT))}
    CCQ_OUT = {t: nc.dram_tensor("ccq_out_" + t,
                                 [n_cores * q * nt * 128, 128], BF16,
                                 kind="Internal",
                                 addr_space="Shared" if n_cores > 4 else "Local").ap()
               for t, q, nt in (("g", QG, FT), ("u", QG, FT), ("d", QD, KT))}
    X_SAVE = nc.dram_tensor("x_save", [D, TOK], F32, kind="Internal").ap()
    ROWRS = nc.dram_tensor("rowrs", [2, DFF], F32, kind="Internal").ap()
    RRS2 = nc.dram_tensor("rrs2", [2, DFF], F32, kind="Internal").ap()
    CC_IN = nc.dram_tensor("cc_in", [H * 128, 132], BF16, kind="Internal").ap()
    CC_OUT = nc.dram_tensor("cc_out", [2 * H * 128, 132], BF16, kind="Internal").ap()
    DBG = {}
    if debug:
        for nm in ("qphi0", "kfm0", "vts0", "ascr00", "yint00", "kv00",
                   "st10", "st1f", "yscr00", "denc00", "yfm0"):
            DBG[nm] = nc.dram_tensor("dbg_" + nm, [128, 2112], F32, kind="Internal").ap()
    groups = [[2 * i, 2 * i + 1] for i in range(max(1, n_cores // 2))]

    es = contextlib.ExitStack()

    def sb(name, p_, f, dt=F32):
        return es.enter_context(nc.sbuf_tensor(name, [p_, f], dt))

    trigc = sb("trigc_t", 128, TOK)
    trigs = sb("trigs_t", 128, TOK)
    maskt = sb("maskt_t", 128, 128)
    id128b = sb("id128b_t", 128, 128, BF16)
    id128f = sb("id128f_t", 128, 128)
    id16f = sb("id16f_t", 16, 16)
    onescol = sb("onescol_t", 128, 1)
    ones128 = sb("ones128_t", 128, 128)
    onesrow = sb("onesrow_t", 1, 128)
    mrecv = sb("mrecv_t", 128, 1)
    bgcol = sb("bgcol", H, 1)
    bocol = sb("bocol", 128, KT)
    wgtil = sb("wgtil", 128, KT * H, BF16)
    rows = sb("rows", 1, 2048)
    colsc = sb("colsc", 128, 64)
    colsc2 = sb("colsc2", 128, 64)
    denc = sb("denc", 128, 8)
    constt = sb("constt_t", 1, 4)
    nc.const_aps.register(F32, 1e-5, constt[0:1, 0:1])
    nc.const_aps.register(F32, 1.0 / 128.0, constt[0:1, 1:2])
    prow = sb("prow", 1, 256)
    yscr = sb("yscr", 128, 132, BF16)

    AR1 = sb("AR1", 128, 22016)
    AR2 = sb("AR2", 128, 16640)
    AR3 = sb("AR3", 128, 8704)
    AR1b = AR1[:].bitcast(BF16)
    AR2b = AR2[:].bitcast(BF16)
    AR3b = AR3[:].bitcast(BF16)

    def f1(a, b_):
        return AR1[:, a:b_]

    wslab = [f1(0, 5504), f1(5504, 11008)]
    awslab = f1(11008, 16512)
    uslab = f1(16512, 22016)
    wqslab = [AR2b[:, 0:5504], AR2b[:, 5504:11008]]
    srow_b = [AR3[0:1, 0:5504], AR3[32:33, 0:5504]]

    t1buf = f1(0, 16384)
    hrbuf_b = AR1b
    g1buf = AR2b[:, 0:16384]
    wbG = [AR2b[:, 16384:18432], AR2b[:, 18432:20480]]
    wbU = [AR2b[:, 20480:22528], AR2b[:, 22528:24576]]
    wbD = [AR2b[:, 16384:21888], AR2b[:, 21888:27392]]
    xtile = [AR2[:, 13696:14720], AR2[:, 14720:15744]]
    tscr = [AR3[:, 0:1024], AR3[:, 1024:2048]]
    amaxt = AR3[:, 2048:3072]
    rowbM = AR3[:, 3072:4096]
    rowbR = AR3[:, 4096:5120]
    rowbS = AR3[:, 5120:6144]
    rowbI = AR3[:, 6144:7168]
    rowbF = AR3[:, 7168:8192]
    recipBq = AR2[:, 5504:11008]
    scaleBq = AR2[:, 11008:16512]

    ropes = [f1(0, 1024), f1(1024, 2048)]
    qfmf = f1(2048, 3072)
    kropef = f1(3072, 4096)
    pcol = f1(4096, 4224)
    rpcol = f1(4224, 4352)
    rpscr = f1(4352, 4480)
    slast = f1(4480, 4608)
    kfm = AR1b[:, 9216:11264]
    wproj_bufs = [AR1b[:, 11264:13312], AR1b[:, 40896:42944]]
    state1 = AR1b[:, 13312:15424]
    staterecv = AR1b[:, 15424:17536]
    state2 = AR1b[:, 17536:19648]
    beta = AR1[0:16, 9824:10848]
    pbuf = AR1[0:16, 10848:11872]
    z16 = AR1[0:16, 11872:12000]
    yint = AR1b[:, 24000:40896]
    xh = AR2b[:, 0:16384]
    kvbuf = AR2b[:, 16384:33280]
    qphi = AR3b[:, 0:16384]

    pA = es.enter_context(nc.psum_tensor("pA", [128, 2048], F32))
    pB = es.enter_context(nc.psum_tensor("pB", [128, 2048], F32))

    def wA(i, w=512, p0=0, p1=128):
        return pA[p0:p1, i * 512:i * 512 + w]

    def wB(i, w=512, p0=0, p1=128):
        return pB[p0:p1, i * 512:i * 512 + w]

    pB_tpv = pB[:, 1024:1088].bitcast(BF16)
    pB_tpy = pB[:, 512:576].bitcast(BF16)

    sems = {n: es.enter_context(nc.semaphore("sem_" + n))
            for n in (["d", "g2d", "cc", "t", "v", "s", "g",
                       "dx0", "dx1"] + ["dw%d" % i for i in range(8)] +
                      ["dsa", "dsb", "dqr", "dbc", "dcc", "dst"] +
                      ["g2d%s" % ch for ch in "abcdefgh"] +
                      ["b" + e for e in "tvsg"])}
    YDMA_SEMS = (["d", "dx0", "dx1"] + ["dw%d" % i for i in range(8)] +
                 ["dsa", "dsb", "dbc", "dcc", "dst"])

    def st_sl(buf, h_):
        return buf[:, h_ * 132:h_ * 132 + 129]

    def kv_sl(h_, c_):
        return kvbuf[:, (h_ * NCH + c_) * 132:(h_ * NCH + c_) * 132 + 129]

    def yint_sl(h_, c_):
        return yint[:, (h_ * NCH + c_) * 132:(h_ * NCH + c_) * 132 + 129]

    def idx(c_, h_):
        return c_ * 16 + h_

    def script(p: P):
        c = p.cnt

        def dump(nm, ap, eng="v"):
            if not debug:
                return
            p.drain(eng)
            p.do(eng, lambda h: h.sem_inc(p.sems[eng], 1))
            c[eng] += 1
            p.wait("y", eng, c[eng])
            n = ap.shape[-1]
            np_ = ap.shape[0]
            p.dma("y", DBG[nm][0:np_, 0:n], ap, sem="d")
            p.wait("y", "d", c["d"])
            p.wait(eng, "d", c["d"])

        def wait_all_dma(eng):
            for s_ in YDMA_SEMS:
                p.wait(eng, s_, c[s_])
            p.wait(eng, "g2d", c["g2d"])
            for ch_ in "abcdefgh":
                p.wait(eng, "g2d" + ch_, c["g2d" + ch_])
            p.wait(eng, "dqr", c["dqr"])

        def recip_acc(out_ap, in_ap, scr_ap, sem=None):
            return p.do("v", lambda h: h.reciprocal(out_ap, in_ap), sem)

        for dst, src in ((trigc, TRIGC), (trigs, TRIGS), (maskt, MASKT),
                         (id128b, ID128B), (id128f, ID128F), (id16f, ID16F),
                         (onescol, ONESCOL), (ones128, ONES128),
                         (constt, CONSTT),
                         (onesrow, ONESROW), (mrecv, MRECV), (bgcol, BG),
                         (bocol, BO)):
            p.dma("y", dst[:], src)
        p.do("g", lambda h: h.dma_start(
            out=wgtil[:].rearrange("p (k hh) -> p k hh", k=KT),
            in_=WG.rearrange("(k p) hh -> p k hh", p=128)), "g2d", 16)
        ld0, g2d0 = c["d"], c["g2d"]
        for e in "tvsg":
            p.wait(e, "d", ld0)
            p.wait(e, "g2d", g2d0)
        p.wait("y", "g2d", g2d0)

        def quant(tag, n_groups, n_out):
            WSRC = WFFN[tag]
            nspan = (n_out + 2047) // 2048
            ncc = n_out // 128
            CIN4 = CCQ_IN[tag].rearrange("(l ot p) m -> l ot p m",
                                         ot=ncc, p=128)
            vdone, wqd, chain_d, scd, lds, absd = {}, {}, {}, {}, {}, {}
            span_rd, span_bc, srow_s = {}, {}, {}
            win_ev = {0: 0, 1: 0, 2: 0, 3: 0}
            q4 = [n_out // 4 * i for i in range(4)] + [n_out]

            def sA_load(g):
                buf = g % 2
                dss = "dsa" if buf == 0 else "dsb"
                if g >= 2:
                    p.wait("y", "v", vdone[g - 2])
                    p.wait("y", dss, wqd.get(g - 2, 0))
                src_ = WSRC.rearrange("(g p) o -> g p o", p=128)[g]
                for i in range(4):
                    p.dma("y", wslab[buf][:, q4[i]:q4[i + 1]],
                          src_[:, q4[i]:q4[i + 1]], sem="dw%d" % (4 * buf + i))
                lds[g] = [("dw%d" % (4 * buf + i), c["dw%d" % (4 * buf + i)])
                          for i in range(4)]

            def sB_bcast(g):
                buf = g % 2
                p.wait("s", "dqr", chain_d[g])
                for sp in range(nspan):
                    o0 = sp * 2048
                    wsp = min(2048, n_out - o0)
                    p.wait("s", "v", span_rd.get((g - 1, sp), 0))
                    p.wait("s", "dbc", c["dbc"])
                    p.do("s", lambda h, b=buf, o_=o0, w_=wsp: h.dma_start(
                        out=recipBq[:, o_:o_ + w_],
                        in_=ROWRS[b:b + 1, o_:o_ + w_].partition_broadcast(128)[:, 0, :]),
                        "dbc", 16)
                    p.do("s", lambda h, b=buf, o_=o0, w_=wsp: h.dma_start(
                        out=scaleBq[:, o_:o_ + w_],
                        in_=RRS2[b:b + 1, o_:o_ + w_].partition_broadcast(128)[:, 0, :]),
                        "dbc", 16)
                    span_bc[(g, sp)] = c["dbc"]

            def sB_spans(g):
                buf = g % 2
                dss = "dsa" if buf == 0 else "dsb"
                for sp in range(nspan):
                    o0 = sp * 2048
                    wsp = min(2048, n_out - o0)
                    if sp == 0 and g >= 2:
                        p.wait("v", dss, wqd[g - 2])
                    p.wait("v", "dbc", span_bc[(g, sp)])
                    if sp == 0:
                        for s_, v_ in lds[g]:
                            p.wait("v", s_, v_)
                    p.drain("v")
                    p.do("v", lambda h, b=buf, o_=o0, w_=wsp: h.tensor_tensor(
                        uslab[:, o_:o_ + w_],
                        wslab[b][:, o_:o_ + w_], recipBq[:, o_:o_ + w_],
                        ALU.mult))
                    p.drain("v")
                    p.do("v", lambda h, o_=o0, w_=wsp: h.tensor_scalar(
                        uslab[:, o_:o_ + w_], uslab[:, o_:o_ + w_],
                        MAGIC, MAGIC, ALU.add, ALU.subtract))
                    p.drain("v")
                    p.do("v", lambda h, o_=o0, w_=wsp: h.tensor_scalar(
                        uslab[:, o_:o_ + w_], uslab[:, o_:o_ + w_],
                        -1.0, 1.0, ALU.max, ALU.min))
                    p.drain("v")
                    p.do("v", lambda h, b=buf, o_=o0, w_=wsp: h.tensor_tensor(
                        wqslab[b][:, o_:o_ + w_], uslab[:, o_:o_ + w_],
                        scaleBq[:, o_:o_ + w_], ALU.mult), "v")
                    span_rd[(g, sp)] = c["v"]
                vdone[g] = c["v"]

            def sA_abs(g):
                buf = g % 2
                for s_, v_ in lds[g]:
                    p.wait("s", s_, v_)
                p.wait("s", "t", scd.get(g - 1, 0))
                p.drain("s")
                p.do("s", lambda h, b=buf, no=n_out: h.activation(
                    awslab[:, 0:no], wslab[b][:, 0:no], ACT.Abs), "s")
                absd[g] = c["s"]

            def sA_scales(g):
                buf = g % 2
                srow = srow_b[buf]
                p.wait("t", "s", absd[g])
                for oc in range((n_out + 511) // 512):
                    cw = min(512, n_out - oc * 512)
                    p.wait("t", "s", win_ev[oc % 4])
                    p.mm(wA(oc % 4, cw, p1=1), onescol[:],
                         awslab[:, oc * 512:oc * 512 + cw])
                    tmm = c["t"]
                    p.wait("s", "t", tmm)
                    p.drain("s")
                    p.do("s", lambda h, o_=oc, w0=cw, sr=srow: h.activation(
                        sr[0:1, o_ * 512:o_ * 512 + w0], wA(o_ % 4, w0, p1=1),
                        ACT.Copy, bias=1e-5, scale=1.0 / 128.0), "s")
                    win_ev[oc % 4] = c["s"]
                scd[g] = c["t"]
                srow_s[g] = c["s"]

            def sA_chain(g):
                buf = g % 2
                srow = srow_b[buf]
                p.wait("y", "s", srow_s[g])
                p.wait("y", "dqr", c["dqr"])
                p.do("y", lambda h, no=n_out, b=buf, sr=srow: h.dma_start(
                    out=ROWRS[b:b + 1, 0:no], in_=sr[0:1, 0:no]), "dqr", 16)
                p.do("y", lambda h, no=n_out, b=buf, sr=srow: h.dma_start(
                    out=RRS2[b:b + 1, 0:no], in_=sr[0:1, 0:no]), "dqr", 16)
                p.wait("y", "dqr", c["dqr"])
                cbuf = colsc if buf == 0 else colsc2
                p.do("y", lambda h, no=n_out, n=ncc, b=buf, cb=cbuf: h.dma_start(
                    out=cb[:, 0:n],
                    in_=ROWRS[b:b + 1, 0:no].rearrange(
                        "one (pp f) -> (one pp) f", pp=128)), "dqr", 16)
                d2 = c["dqr"]
                p.wait("v", "dqr", d2)
                p.drain("v")
                p.do("v", lambda h, n=ncc, cb=cbuf: h.reciprocal(
                    cb[:, 0:n], cb[:, 0:n]), "v")
                rec_v = c["v"]
                p.wait("y", "v", rec_v)
                p.do("y", lambda h, no=n_out, n=ncc, b=buf, cb=cbuf: h.dma_start(
                    out=ROWRS[b:b + 1, 0:no].rearrange(
                        "one (pp f) -> (one pp) f", pp=128),
                    in_=cb[:, 0:n]), "dqr", 16)
                chain_d[g] = c["dqr"]

            def sB_store(g):
                buf = g % 2
                dss = "dsa" if buf == 0 else "dsb"
                p.wait("y", "v", vdone[g])
                for ot in range(ncc):
                    p.dma("y", CIN4[g, ot],
                          wqslab[buf][:, ot * 128:(ot + 1) * 128], sem=dss)
                wqd[g] = c[dss]

            for it in range(n_groups + 1):
                if it < n_groups:
                    sA_load(it)
                if it >= 1:
                    sA_chain(it - 1)
                    sB_bcast(it - 1)
                    sB_spans(it - 1)
                if it < n_groups:
                    sA_abs(it)
                    sA_scales(it)
                if it >= 1:
                    sB_store(it - 1)
            p.wait("y", "dsa", c["dsa"])
            p.wait("y", "dsb", c["dsb"])
            p.wait("y", "v", vdone[n_groups - 1])

        allg = [[i for i in range(n_cores)]]
        ccq_done = {}

        def issue_cc(t):
            p.wait("g", "dsa", c["dsa"])
            p.wait("g", "dsb", c["dsb"])
            p.do("g", lambda h, t_=t: h.collective_compute(
                "AllGather", ALU.bypass, ins=[CCQ_IN[t_]], outs=[CCQ_OUT[t_]],
                replica_groups=allg), "cc", 1)
            ccq_done[t] = c["cc"]

        quant("g", QG, DFF)
        issue_cc("g")
        quant("u", QG, DFF)
        issue_cc("u")
        quant("d", QD, D)
        issue_cc("d")
        p.barrier()

        def ln_stats(XSRC):
            wait_all_dma("y")
            for k in range(KT):
                xb = xtile[k % 2]
                dxs = "dx0" if k % 2 == 0 else "dx1"
                if k == 0:
                    p.wait("y", "t", c["t"])
                    p.wait("y", "v", c["v"])
                    p.wait("y", "s", c["s"])
                if k >= 2:
                    p.wait("y", "t", c["t"])
                p.dma("y", xb[:], XSRC.rearrange("(k p) t -> k p t", p=128)[k],
                      sem=dxs)
                ldx = c[dxs]
                sq = tscr[k % 2]
                p.wait("s", dxs, ldx)
                if k >= 2:
                    p.wait("s", "t", c["t"])
                p.drain("s")
                sk = p.do("s", lambda h, b=xb, q=sq: h.activation(
                    q[:, 0:TOK], b[:], ACT.Square), "s")
                p.wait("t", dxs, ldx)
                p.wait("t", "s", sk)
                if k == 0:
                    p.wait("t", "v", c["v"])
                first, last = (k == 0), (k == KT - 1)
                for nh in range(2):
                    p.mm(wA(0 + nh, p1=1), onescol[:],
                         xb[:, nh * 512:(nh + 1) * 512], start=first, stop=last)
                    p.mm(wA(2 + nh, p1=1), onescol[:],
                         sq[:, nh * 512:(nh + 1) * 512], start=first, stop=last)
            tdone = c["t"]
            p.wait("v", "t", tdone)
            p.wait("v", "dst", c["dst"])
            p.drain("v")
            for nh in range(2):
                p.do("v", lambda h, n=nh: h.tensor_scalar(
                    rows[0:1, n * 512:(n + 1) * 512], wA(n, p1=1),
                    1.0 / D, None, ALU.mult))
                p.do("v", lambda h, n=nh: h.tensor_scalar(
                    tscr[0][0:1, n * 512:(n + 1) * 512], wA(2 + n, p1=1),
                    1.0 / D, None, ALU.mult))
            p.drain("v")
            p.do("v", lambda h: h.tensor_tensor(
                tscr[1][0:1, 0:TOK], rows[0:1, 0:TOK], rows[0:1, 0:TOK], ALU.mult))
            p.drain("v")
            p.do("v", lambda h: h.tensor_tensor(
                tscr[0][0:1, 0:TOK], tscr[0][0:1, 0:TOK], tscr[1][0:1, 0:TOK],
                ALU.subtract), "v")
            p.drain("v")
            p.do("v", lambda h: h.tensor_scalar(
                tscr[0][0:1, 0:TOK], tscr[0][0:1, 0:TOK], 1e-5, None, ALU.add), "v")
            vv = c["v"]
            p.wait("s", "v", vv)
            p.drain("s")
            p.do("s", lambda h: h.activation(
                tscr[0][0:1, 0:TOK], tscr[0][0:1, 0:TOK], ACT.Ln), "s")
            sln = c["s"]
            p.wait("v", "s", sln)
            p.drain("v")
            p.do("v", lambda h: h.tensor_scalar(
                tscr[0][0:1, 0:TOK], tscr[0][0:1, 0:TOK], -0.5, None, ALU.mult), "v")
            vln = c["v"]
            p.wait("s", "v", vln)
            p.drain("s")
            p.do("s", lambda h: h.activation(
                rows[0:1, 1024:2048], tscr[0][0:1, 0:TOK], ACT.Exp), "s")
            sR = c["s"]
            p.wait("v", "s", sR)
            p.wait("t", "s", sR)

        def bcast(row_ap, win, dst):
            n = row_ap.shape[-1]
            p.wait("y", "v", c["v"])
            p.wait("y", "s", c["s"])
            p.wait("y", "dbc", c["dbc"])
            p.dma("y", RRS2[0:1, 0:n], row_ap, sem="dbc")
            p.wait("y", "dbc", c["dbc"])
            bc = p.dma("y", dst, RRS2[0:1, 0:n].partition_broadcast(128)[:, 0, :],
                       sem="dbc")
            return bc

        def amax_bc(dst_rowb, mul_rowb, mul_const):
            p.wait("t", "v", c["v"])
            for b_ in range(8):
                p.tp(pA[:, b_ * 128:(b_ + 1) * 128],
                     amaxt[:, b_ * 128:(b_ + 1) * 128], id128f[:])
            tpd = c["t"]
            p.wait("v", "t", tpd)
            p.drain("v")
            p.do("v", lambda h: h.tensor_copy(tscr[1][:, 0:TOK], pA[:, 0:TOK]))
            p.drain("v")
            p.do("v", lambda h: h.tensor_reduce(
                colsc[:, 0:8],
                tscr[1][:, 0:TOK].rearrange("p (b f) -> p b f", f=128),
                axis=mybir.AxisListType.X, op=ALU.max,
                apply_absolute_value=True))
            p.drain("v")
            p.do("v", lambda h: h.tensor_scalar(
                colsc[:, 0:8], colsc[:, 0:8], 1e-5, None, ALU.max))
            p.drain("v")
            recip_acc(colsc[:, 8:16], colsc[:, 0:8], colsc[:, 24:32])
            p.drain("v")
            p.do("v", lambda h: h.tensor_scalar(
                colsc[:, 16:24], colsc[:, 8:16], 127.0, None, ALU.mult), "v")
            for b_ in range(8):
                p.drain("v")
                p.do("v", lambda h, bb=b_: h.tensor_scalar(
                    tscr[1][:, 0:128], id128f[:], colsc[:, 16 + bb:17 + bb], None,
                    ALU.mult))
                p.drain("v")
                p.do("v", lambda h, bb=b_: h.tensor_scalar(
                    tscr[1][:, 128:256], id128f[:], colsc[:, 0 + bb:1 + bb], None,
                    ALU.mult), "v")
                dgv = c["v"]
                p.wait("t", "v", dgv)
                p.mm(pA[:, 1024 + b_ * 128:1152 + b_ * 128], ones128[:],
                     tscr[1][:, 0:128])
                tmm = p.mm(pB[:, b_ * 128:(b_ + 1) * 128], ones128[:],
                           tscr[1][:, 128:256])
                p.wait("v", "t", tmm)
            p.drain("v")
            p.do("v", lambda h: h.tensor_copy(rowbS[:, 0:TOK], pA[:, 1024:2048]))
            p.drain("v")
            p.do("v", lambda h: h.scalar_tensor_tensor(
                dst_rowb[:, 0:TOK], pB[:, 0:TOK], mul_const,
                mul_rowb[:, 0:TOK], ALU.mult, ALU.mult), "v")

        def ffn(XSRC, XDST):
            XS3 = XSRC.rearrange("(k p) t -> k p t", p=128)
            XD3 = XDST.rearrange("(k p) t -> k p t", p=128)
            gg4 = CCQ_OUT["g"].rearrange("(s ot p) m -> s ot p m", ot=FT, p=128)
            uu4 = CCQ_OUT["u"].rearrange("(s ot p) m -> s ot p m", ot=FT, p=128)
            dd4 = CCQ_OUT["d"].rearrange("(s ot p) m -> s ot p m", ot=KT, p=128)
            ln_stats(XSRC)
            bcast(rows[0:1, 0:TOK], None, rowbM[:, 0:TOK])
            bcR = bcast(rows[0:1, 1024:2048], None, rowbR[:, 0:TOK])
            p.wait("v", "dbc", bcR)
            t1v, absS, fold = {}, {}, {}

            def emit_fold(k):
                p.wait("v", "s", absS[k])
                p.drain("v")
                q = tscr[k % 2]
                if k == 0:
                    p.do("v", lambda h, q_=q: h.tensor_copy(
                        amaxt[:, 0:TOK], q_[:, 0:TOK]), "v")
                else:
                    p.do("v", lambda h, q_=q: h.tensor_tensor(
                        amaxt[:, 0:TOK], amaxt[:, 0:TOK], q_[:, 0:TOK],
                        ALU.max), "v")
                fold[k] = c["v"]

            for k in range(KT):
                xb = xtile[k % 2]
                dxs = "dx0" if k % 2 == 0 else "dx1"
                if k == 0:
                    p.wait("y", "v", c["v"])
                    p.wait("y", "s", c["s"])
                if k >= 2:
                    p.wait("y", "v", t1v[k - 2])
                p.dma("y", xb[:], XS3[k], sem=dxs)
                ldx = c[dxs]
                p.wait("v", dxs, ldx)
                p.drain("v")
                p.do("v", lambda h, b=xb, kk=k: h.scalar_tensor_tensor(
                    t1buf[:, kk * TOK:(kk + 1) * TOK], b[:], 0.0,
                    rowbM[:, 0:TOK], ALU.add, ALU.subtract), "v")
                t1v[k] = c["v"]
                p.wait("s", "v", t1v[k])
                if k >= 2:
                    p.wait("s", "v", fold[k - 2])
                p.drain("s")
                absS[k] = p.do("s", lambda h, q=tscr[k % 2], kk=k: h.activation(
                    q[:, 0:TOK], t1buf[:, kk * TOK:(kk + 1) * TOK], ACT.Abs),
                    "s")
                if k >= 1:
                    emit_fold(k - 1)
            emit_fold(KT - 1)
            amax_bc(rowbI, rowbR, 1.0 / 127.0)
            g1v = {}
            for k in range(KT):
                p.drain("v")
                p.do("v", lambda h, kk=k: h.tensor_tensor(
                    tscr[0][:, 0:TOK], t1buf[:, kk * TOK:(kk + 1) * TOK],
                    rowbS[:, 0:TOK], ALU.mult))
                p.drain("v")
                p.do("v", lambda h, kk=k: h.tensor_scalar(
                    g1buf[:, kk * TOK:(kk + 1) * TOK], tscr[0][:, 0:TOK],
                    MAGIC, MAGIC, ALU.add, ALU.subtract), "v")
                g1v[k] = c["v"]

            gld, gumm, epd, hrv, absS2, fold2 = {}, {}, {}, {}, {}, {}

            def gu_load(og):
                pg = og % 2
                if og == 0:
                    p.wait("y", "cc", ccq_done["u"])
                if og >= 2:
                    p.wait("y", "t", gumm[og - 2])
                for i, (buf, W4) in enumerate(((wbG[pg], gg4), (wbU[pg], uu4))):
                    for hf in range(2):
                        s_ = "dw%d" % (pg * 4 + i * 2 + hf)
                        for g in range(hf * 8, (hf + 1) * 8):
                            p.dma("y", buf[:, g * 128:(g + 1) * 128],
                                  W4[g, og], sem=s_)
                gld[og] = [("dw%d" % (pg * 4 + j), c["dw%d" % (pg * 4 + j)])
                           for j in range(4)]

            def emit_fold2(og):
                p.wait("s", "v", c["v"])
                if og >= 1:
                    p.wait("s", "v", fold2.get(og - 1, 0))
                p.drain("s")
                absS2[og] = p.do("s", lambda h, o=og: h.activation(
                    tscr[1][:, 0:TOK],
                    hrbuf_b[:, o * TOK:(o + 1) * TOK], ACT.Abs), "s")
                p.wait("v", "s", absS2[og])
                p.drain("v")
                if og == 0:
                    p.do("v", lambda h: h.tensor_copy(
                        amaxt[:, 0:TOK], tscr[1][:, 0:TOK]), "v")
                else:
                    p.do("v", lambda h: h.tensor_tensor(
                        amaxt[:, 0:TOK], amaxt[:, 0:TOK], tscr[1][:, 0:TOK],
                        ALU.max), "v")
                fold2[og] = c["v"]

            gu_load(0)
            for og in range(FT):
                if og + 1 < FT:
                    gu_load(og + 1)
                pg = og % 2
                for s_, v_ in gld[og]:
                    p.wait("t", s_, v_)
                if og >= 2:
                    p.wait("t", "v", epd[og - 2])
                for g in range(KT):
                    if og == 0:
                        p.wait("t", "v", g1v[g])
                    for ch in range(2):
                        p.mm(pA[:, pg * 1024 + ch * 512:pg * 1024 + ch * 512 + 512],
                             wbG[pg][:, g * 128:(g + 1) * 128],
                             g1buf[:, g * TOK + ch * 512:g * TOK + ch * 512 + 512],
                             start=(g == 0), stop=(g == KT - 1))
                    for ch in range(2):
                        p.mm(pB[:, pg * 1024 + ch * 512:pg * 1024 + ch * 512 + 512],
                             wbU[pg][:, g * 128:(g + 1) * 128],
                             g1buf[:, g * TOK + ch * 512:g * TOK + ch * 512 + 512],
                             start=(g == 0), stop=(g == KT - 1))
                gumm[og] = c["t"]
                p.wait("v", "t", gumm[og])
                p.wait("v", "s", c["s"])
                p.drain("v")
                p.do("v", lambda h, g_=pg: h.scalar_tensor_tensor(
                    tscr[0][:, 0:TOK], pA[:, g_ * 1024:(g_ + 1) * 1024], 0.0,
                    rowbI[:, 0:TOK], ALU.add, ALU.mult), "v")
                gtv = c["v"]
                p.wait("s", "v", gtv)
                if og >= 1:
                    p.wait("s", "v", fold2.get(og - 1, 0))
                p.drain("s")
                sil = p.do("s", lambda h: h.activation(
                    tscr[1][:, 0:TOK], tscr[0][:, 0:TOK], ACT.Silu), "s")
                p.wait("v", "s", sil)
                p.drain("v")
                p.do("v", lambda h, g_=pg, o=og: h.scalar_tensor_tensor(
                    hrbuf_b[:, o * TOK:(o + 1) * TOK],
                    pB[:, g_ * 1024:(g_ + 1) * 1024], 0.0,
                    tscr[1][:, 0:TOK], ALU.add, ALU.mult), "v")
                hrv[og] = c["v"]
                epd[og] = c["v"]
                if og >= 1:
                    emit_fold2(og - 1)
            emit_fold2(FT - 1)
            amax_bc(rowbF, rowbI, 0.5 / 127.0)
            g2v = {}
            for f in range(FT):
                p.drain("v")
                p.do("v", lambda h, ff=f: h.tensor_tensor(
                    tscr[0][:, 0:TOK], hrbuf_b[:, ff * TOK:(ff + 1) * TOK],
                    rowbS[:, 0:TOK], ALU.mult))
                p.drain("v")
                p.do("v", lambda h, ff=f: h.tensor_scalar(
                    hrbuf_b[:, ff * TOK:(ff + 1) * TOK], tscr[0][:, 0:TOK],
                    MAGIC, MAGIC, ALU.add, ALU.subtract), "v")
                g2v[f] = c["v"]

            dld, dmm, ep2, dstc = {}, {}, {}, {}
            qr = [0, 11, 22, 33, FT]

            def d_load(og):
                pg = og % 2
                if og < 2:
                    p.wait("y", "t", gumm[FT - 1])
                else:
                    p.wait("y", "t", dmm[og - 2])
                if og == 0:
                    p.wait("y", "cc", ccq_done["d"])
                for i in range(4):
                    s_ = "dw%d" % (pg * 4 + i)
                    for g in range(qr[i], qr[i + 1]):
                        p.dma("y", wbD[pg][:, g * 128:(g + 1) * 128],
                              dd4[d_slot[g], og], sem=s_)
                dld[og] = [("dw%d" % (pg * 4 + j), c["dw%d" % (pg * 4 + j)])
                           for j in range(4)]

            d_load(0)
            for og in range(KT):
                if og + 1 < KT:
                    d_load(og + 1)
                pg = og % 2
                for s_, v_ in dld[og]:
                    p.wait("t", s_, v_)
                if og >= 2:
                    p.wait("t", "v", ep2[og - 2])
                for g in range(FT):
                    if og == 0:
                        p.wait("t", "v", g2v[g])
                    for ch in range(2):
                        p.mm(pA[:, pg * 1024 + ch * 512:pg * 1024 + ch * 512 + 512],
                             wbD[pg][:, g * 128:(g + 1) * 128],
                             hrbuf_b[:, g * TOK + ch * 512:g * TOK + ch * 512 + 512],
                             start=(g == 0), stop=(g == FT - 1))
                dmm[og] = c["t"]
                xb = xtile[og % 2]
                dxs = "dx0" if og % 2 == 0 else "dx1"
                if og == 0:
                    p.wait("y", "v", c["v"])
                if og >= 2:
                    p.wait("y", "v", ep2[og - 2])
                    p.wait("y", "dst" if og % 2 == 0 else "d", dstc[og - 2])
                p.dma("y", xb[:], XS3[og], sem=dxs)
                ldx = c[dxs]
                p.wait("v", "t", dmm[og])
                p.wait("v", dxs, ldx)
                p.drain("v")
                p.do("v", lambda h, g_=pg: h.scalar_tensor_tensor(
                    tscr[0][:, 0:TOK], pA[:, g_ * 1024:(g_ + 1) * 1024], 0.0,
                    rowbF[:, 0:TOK], ALU.add, ALU.mult))
                p.drain("v")
                p.do("v", lambda h, b=xb: h.tensor_tensor(
                    b[:], b[:], tscr[0][:, 0:TOK], ALU.add), "v")
                ep2[og] = c["v"]
                p.wait("y", "v", ep2[og])
                sst = "dst" if og % 2 == 0 else "d"
                p.dma("y", XD3[og], xb[:], sem=sst)
                dstc[og] = c[sst]
            p.barrier()

        if "ffn1" in stages:
            ffn(X, X_SAVE)
        if "attn" not in stages:
            wait_all_dma("y")
            return

        ln_stats(X_SAVE)
        bcast(rows[0:1, 0:TOK], None, rowbM[:, 0:TOK])
        bc4 = bcast(rows[0:1, 1024:2048], None, rowbR[:, 0:TOK])
        p.wait("v", "dbc", bc4)
        for k in range(KT):
            xb = xtile[k % 2]
            dxs = "dx0" if k % 2 == 0 else "dx1"
            if k == 0:
                p.wait("y", "t", c["t"])
                p.wait("y", "v", c["v"])
                p.wait("v", "dst", c["dst"])
            if k >= 2:
                p.wait("y", "v", c["v"])
            p.dma("y", xb[:], X_SAVE.rearrange("(k p) t -> k p t", p=128)[k],
                  sem=dxs)
            ldx = c[dxs]
            p.wait("v", dxs, ldx)
            p.drain("v")
            p.do("v", lambda h, b=xb: h.scalar_tensor_tensor(
                tscr[0][:, 0:TOK], b[:], 0.0, rowbM[:, 0:TOK],
                ALU.add, ALU.subtract))
            p.drain("v")
            p.do("v", lambda h, kk=k: h.tensor_tensor(
                xh[:, kk * TOK:(kk + 1) * TOK], tscr[0][:, 0:TOK],
                rowbR[:, 0:TOK], ALU.mult), "v")
        xh_done = c["v"]
        p.wait("t", "v", xh_done)
        p.wait("g", "v", xh_done)

        for nh in range(2):
            for k in range(KT):
                p.mm(wA(nh, p1=16), wgtil[:, k * 16:(k + 1) * 16],
                     xh[:, k * TOK + nh * 512:k * TOK + (nh + 1) * 512],
                     start=(k == 0), stop=(k == KT - 1))
        bmm = c["t"]
        p.wait("s", "t", bmm)
        p.drain("s")
        for nh in range(2):
            p.do("s", lambda h, n=nh: h.activation(
                beta[:, n * 512:(n + 1) * 512], wA(n, p1=16), ACT.Sigmoid,
                bias=bgcol[:], scale=1.0), "s")
        sbeta = c["s"]
        p.wait("v", "s", sbeta)
        p.drain("v")
        p.do("v", lambda h: h.tensor_scalar(
            beta[:, 0:TOK], beta[:, 0:TOK], 0.8, 0.9995, ALU.max, ALU.min))
        p.do("v", lambda h: h.memset(z16[:, 0:128], 0.0))
        p.drain("v")
        for c_ in range(NCH):
            p.do("v", lambda h, cc=c_: h.tensor_tensor_scan(
                pbuf[:, cc * 128:(cc + 1) * 128], beta[:, cc * 128:(cc + 1) * 128],
                z16[:, 0:128], 1.0, ALU.mult, ALU.add), "v")
        pscan = c["v"]
        p.wait("t", "v", pscan)
        for c_ in range(NCH):
            p.tp(pA[:, 1024 + c_ * 16:1024 + (c_ + 1) * 16],
                 pbuf[:, c_ * 128:(c_ + 1) * 128], id16f[:])
        ptp = c["t"]
        p.wait("v", "t", ptp)
        p.drain("v")
        p.do("v", lambda h: h.tensor_copy(pcol[:], pA[:, 1024:1152]), "v")
        p.drain("v")
        recip_acc(rpcol[:], pcol[:], rpscr[:], "v")
        p.do("v", lambda h: h.sem_inc(p.sems["v"], 1))
        c["v"] += 1
        p.wait("g", "v", c["v"])
        p.do("g", lambda h: h.dma_start(
            out=prow[0:1, 0:128], in_=pcol[127:128, :]), "g2d", 16)
        vsl = c["g2d"]
        p.wait("t", "g2d", vsl)
        tm = p.mm(pA[:, 1536:1664], onesrow[:], prow[0:1, 0:128])
        p.wait("v", "t", tm)
        p.wait("v", "dqr", vsl)
        p.drain("v")
        p.do("v", lambda h: h.tensor_copy(slast[:], pA[:, 1536:1664]))
        p.do("v", lambda h: h.memset(state1[:, 0:H * 132], 0.0))
        p.drain("v")
        p.do("v", lambda h: h.memset(prow[0:1, 128:144], 1.0))
        p.drain("v")
        for c_ in range(1, NCH):
            p.do("v", lambda h, cc=c_: h.tensor_tensor(
                prow[0:1, 128 + cc * 16:144 + cc * 16],
                prow[0:1, 112 + cc * 16:128 + cc * 16],
                prow[0:1, cc * 16 - 16:cc * 16], ALU.mult))
            p.drain("v")
        p.do("v", lambda h: h.sem_inc(p.sems["v"], 1))
        c["v"] += 1
        initv = c["v"]
        p.wait("t", "v", initv)
        p.wait("g", "v", initv)

        wp_idx = [0]
        wp_tmm = {}

        def wproj_load(src_ap):
            i_ = wp_idx[0]
            buf = wproj_bufs[i_ % 2]
            if i_ >= 2:
                p.wait("g", "t", wp_tmm[i_ - 2])
            sems_ = []
            for q in range(4):
                s_ = "g2d%s" % ("abcdefgh"[(i_ % 2) * 4 + q])
                p.do("g", lambda h, q_=q: h.dma_start(
                    out=buf[:, q_ * 512:(q_ + 1) * 512].rearrange(
                        "p (k n) -> p k n", k=4),
                    in_=src_ap[:, q_ * 4:(q_ + 1) * 4, :]), s_, 16)
                sems_.append((s_, c[s_]))
            wp_idx[0] += 1
            return buf, sems_, i_

        def wproj_done(i_):
            wp_tmm[i_] = c["t"]

        pw_S = [pB[:, 0:128], pB[:, 512:640]]
        pw_Y = [pB[:, 1024:1153], pB[:, 1536:1665]]
        pw_T = [pA[:, 0:64].bitcast(BF16), pA[:, 1536:1600].bitcast(BF16)]
        pw_K = [pA[:, 512:641], pA[:, 1024:1153]]
        ascrP = [AR3b[:, 16384:16512], AR3b[:, 16512:16640]]
        ksbP = [AR3b[:, 16640:16768], AR3b[:, 16768:16896]]
        fscr = qfmf[:, 0:129]

        def rope_phi(src_psum, dst, sign_h):
            p.drain("v")
            p.do("v", lambda h: h.tensor_tensor(
                ropes[0][0:64, 0:TOK], src_psum[0:64, :], trigc[0:64, :],
                ALU.mult))
            p.do("v", lambda h: h.tensor_tensor(
                ropes[1][0:64, 0:TOK], src_psum[64:128, :], trigs[64:128, :],
                ALU.mult))
            p.do("v", lambda h: h.tensor_tensor(
                ropes[0][64:128, 0:TOK], src_psum[0:64, :], trigs[0:64, :],
                ALU.mult))
            p.do("v", lambda h: h.tensor_tensor(
                ropes[1][64:128, 0:TOK], src_psum[64:128, :], trigc[64:128, :],
                ALU.mult), "v")
            pread = c["v"]
            p.drain("v")
            p.do("v", lambda h: h.tensor_tensor(
                kropef[0:64, 0:TOK], ropes[0][0:64, 0:TOK],
                ropes[1][0:64, 0:TOK], ALU.subtract))
            p.do("v", lambda h: h.tensor_tensor(
                kropef[64:128, 0:TOK], ropes[0][64:128, 0:TOK],
                ropes[1][64:128, 0:TOK], ALU.add))
            p.drain("v")
            p.do("v", lambda h: h.tensor_scalar(
                ropes[0][:, 0:TOK], kropef[:, 0:TOK], 0.0, None, ALU.min), "v")
            mn = c["v"]
            p.wait("s", "v", mn)
            p.drain("s")
            ex = p.do("s", lambda h: h.activation(
                ropes[1][:, 0:TOK], ropes[0][:, 0:TOK], ACT.Exp), "s")
            p.wait("v", "s", ex)
            p.drain("v")
            p.do("v", lambda h: h.scalar_tensor_tensor(
                dst, kropef[:, 0:TOK], 0.0, ropes[1][:, 0:TOK],
                ALU.max, ALU.add), "v")
            return pread, c["v"]

        hend_v = {}
        for h_ in range(H):
            wprojQ, ldq, wpiQ = wproj_load(WT["wq"][h_])
            for s_, v_ in ldq:
                p.wait("t", s_, v_)
            p.wait("t", "v", hend_v.get(h_ - 1, initv))
            for nh in range(2):
                for k in range(KT):
                    p.mm(pA[:, nh * 512:(nh + 1) * 512],
                         wprojQ[:, k * 128:(k + 1) * 128],
                         xh[:, k * TOK + nh * 512:k * TOK + (nh + 1) * 512],
                         start=(k == 0), stop=(k == KT - 1))
            qmm = c["t"]
            wproj_done(wpiQ)
            wprojK, ldk, wpiK = wproj_load(WT["wk"][h_])
            for s_, v_ in ldk:
                p.wait("t", s_, v_)
            for nh in range(2):
                for k in range(KT):
                    p.mm(pA[:, 1024 + nh * 512:1024 + (nh + 1) * 512],
                         wprojK[:, k * 128:(k + 1) * 128],
                         xh[:, k * TOK + nh * 512:k * TOK + (nh + 1) * 512],
                         start=(k == 0), stop=(k == KT - 1))
            kmm = c["t"]
            wproj_done(wpiK)
            wprojV, ldv, wpiV = wproj_load(WT["wv"][h_])
            for s_, v_ in ldv:
                p.wait("t", s_, v_)
            for nh in range(2):
                for k in range(KT):
                    p.mm(pB[:, nh * 512:(nh + 1) * 512],
                         wprojV[:, k * 128:(k + 1) * 128],
                         xh[:, k * TOK + nh * 512:k * TOK + (nh + 1) * 512],
                         start=(k == 0), stop=(k == KT - 1))
            vmm = c["t"]
            wproj_done(wpiV)

            p.wait("v", "t", qmm)
            _, qphi_v = rope_phi(pA[:, 0:1024], qphi[:, h_ * TOK:(h_ + 1) * TOK],
                                 h_)
            p.wait("v", "t", kmm)
            _, kphi_v = rope_phi(pA[:, 1024:2048], kfm[:, 0:TOK], h_)
            if "p1a" in stages and h_ == 0:
                wait_all_dma("y")
                return
            vfm_bf = ropes[0][:].bitcast(BF16)
            vts = kropef[:].bitcast(BF16)
            p.wait("v", "t", vmm)
            p.drain("v")
            p.do("v", lambda h: h.tensor_copy(vfm_bf[:, 0:TOK], pB[:, 0:TOK]))
            p.drain("v")
            p.do("v", lambda h: h.memset(
                vts[:, 0:NCH * 132].rearrange("p (n e) -> p n e", e=132)[:, :, 128:129],
                1.0), "v")
            vts_init = c["v"]
            p.wait("t", "v", vts_init)
            vcp = {}
            for c_ in range(NCH):
                if c_ >= 2:
                    p.wait("t", "v", vcp[c_ - 2])
                ttp = p.tp(pw_T[c_ % 2], vfm_bf[:, c_ * 128:(c_ + 1) * 128],
                           id128b[:])
                p.wait("v", "t", ttp)
                p.drain("v")
                p.do("v", lambda h, cc=c_: h.tensor_copy(
                    vts[:, cc * 132:cc * 132 + 128], pw_T[cc % 2]), "v")
                vcp[c_] = c["v"]
            vts_done = c["v"]
            if "p1b" in stages and h_ == 0:
                wait_all_dma("y")
                return

            mask_v, ks_v, st_v, kvt, ymt = {}, {}, {}, {}, {}
            nch_h = 0 if ("noc%d" % h_) in stages else NCH
            for c_ in range(nch_h + 1 if nch_h else 0):
                if c_ < nch_h:
                    pw = c_ % 2
                    ii = idx(c_, h_)
                    if c_ == 0:
                        p.wait("t", "v", max(qphi_v, kphi_v, vts_done))
                    if "nos" not in stages:
                        if c_ >= 2:
                            p.wait("t", "v", mask_v[c_ - 2])
                        smm = p.mm(pw_S[pw], kfm[:, c_ * 128:(c_ + 1) * 128],
                                   qphi[:, h_ * TOK + c_ * 128:h_ * TOK + (c_ + 1) * 128])
                        p.wait("v", "t", smm)
                        p.drain("v")
                        p.do("v", lambda h, i_=ii, w=pw: h.scalar_tensor_tensor(
                            ascrP[w][:], pw_S[w], rpcol[:, i_:i_ + 1], maskt[:],
                            ALU.mult, ALU.mult), "v")
                        mask_v[c_] = c["v"]
                    else:
                        mask_v[c_] = c["v"]
                    if "nok" not in stages:
                        if c_ >= 2:
                            p.wait("t", "v", ks_v[c_ - 2])
                        ktp = p.tp(pw_T[pw], kfm[:, c_ * 128:(c_ + 1) * 128],
                                   id128b[:])
                        p.wait("v", "t", ktp)
                        p.drain("v")
                        p.do("v", lambda h, i_=ii, w=pw: h.tensor_scalar(
                            ksbP[w][:], pw_T[w], rpcol[:, i_:i_ + 1], None,
                            ALU.mult), "v")
                        ks_v[c_] = c["v"]
                    else:
                        ks_v[c_] = c["v"]
                if c_ >= 1:
                    cc = c_ - 1
                    pw = cc % 2
                    ii = idx(cc, h_)
                    if "nokv" not in stages:
                        p.wait("t", "v", ks_v[cc])
                        if cc >= 2:
                            p.wait("t", "v", st_v[cc - 2])
                        kvmm = p.mm(pw_K[pw], ksbP[pw][:],
                                    vts[:, cc * 132:cc * 132 + 129])
                        kvt[cc] = c["t"]
                    if "noy" not in stages:
                        if cc >= 1:
                            p.wait("t", "v", st_v[cc - 1])
                        p.mm(pw_Y[pw], ascrP[pw][:],
                             vts[:, cc * 132:cc * 132 + 129],
                             start=True, stop=False)
                        p.mm(pw_Y[pw],
                             qphi[:, h_ * TOK + cc * 128:h_ * TOK + (cc + 1) * 128],
                             st_sl(state1, h_), start=False, stop=True)
                        ymt[cc] = c["t"]
                    if "nokv" not in stages:
                        p.wait("v", "t", kvt[cc])
                        p.drain("v")
                        p.do("v", lambda h, hh=h_, ci=cc: h.tensor_copy(
                            kv_sl(hh, ci), pw_K[ci % 2]))
                        p.drain("v")
                        p.do("v", lambda h, hh=h_, ci=cc: h.tensor_tensor(
                            fscr, st_sl(state1, hh), kv_sl(hh, ci), ALU.add), "v")
                        if "noy" not in stages:
                            p.wait("v", "t", ymt[cc])
                        p.drain("v")
                        p.do("v", lambda h, hh=h_, i_=ii: h.tensor_scalar(
                            st_sl(state1, hh), fscr, slast[:, i_:i_ + 1], None,
                            ALU.mult), "v")
                        p.drain("v")
                    if "noy" not in stages:
                        p.wait("v", "t", ymt[cc])
                        p.drain("v")
                        p.do("v", lambda h, hh=h_, ci=cc: h.tensor_copy(
                            yint_sl(hh, ci), pw_Y[ci % 2]), "v")
                    st_v[cc] = c["v"]
            hend_v[h_] = c["v"]
            if "p1h%d" % h_ in stages:
                wait_all_dma("y")
                return
        if "p1" in stages:
            wait_all_dma("y")
            return

        p.drain("v")
        p.do("v", lambda h: h.sem_inc(p.sems["bv"], 1))
        c["bv"] += 1
        p.wait("y", "bv", c["bv"])
        p.dma("y", CC_IN.rearrange("(hh pp) e -> pp hh e", pp=128),
              state1[:, 0:H * 132].rearrange("p (hh e) -> p hh e", e=132),
              sem="dcc")
        ccin_d = c["dcc"]
        p.wait("g", "dcc", ccin_d)
        p.do("g", lambda h: h.collective_compute(
            "AllGather", ALU.bypass, ins=[CC_IN], outs=[CC_OUT],
            replica_groups=groups), "cc", 1)
        p.wait("y", "cc", c["cc"])
        p.dma("y", staterecv[:, 0:H * 132].rearrange("p (hh e) -> p hh e", e=132),
              CC_OUT[0:H * 128, :].rearrange("(hh pp) e -> pp hh e", pp=128),
              sem="dcc")
        ccout_d = c["dcc"]
        p.wait("v", "dcc", ccout_d)
        p.drain("v")
        p.do("v", lambda h: h.tensor_scalar(
            state2[:, 0:H * 132], staterecv[:, 0:H * 132], mrecv[:], None,
            ALU.mult), "v")
        s2init = c["v"]
        p.wait("t", "v", s2init)
        if "cc1" in stages:
            wait_all_dma("y")
            return

        yfm = xh
        picol = ropes[0]
        tmpi = p.mm(pA[:, 1536:1664], onesrow[:], prow[0:1, 128:256])
        p.wait("v", "t", tmpi)
        p.drain("v")
        p.do("v", lambda h: h.tensor_copy(picol[:, 0:128], pA[:, 1536:1664]), "v")
        pic_v = c["v"]
        p.wait("t", "v", pic_v)
        uP = [qfmf[:, 256:388], qfmf[:, 388:520]]
        ytmpP = ksbP
        pw_tp = [pB_tpy, pB_tpv]
        order2 = [(hh, cc) for hh in range(H) for cc in range(NCH)]
        immt, ytp_t, yn_v = {}, {}, {}
        for j, (h_, c_) in enumerate(order2):
            ii = idx(c_, h_)
            pw = j % 2
            if j >= 2:
                p.wait("t", "v", yn_v[j - 2])
            p.mm(pA[:, pw * 512:pw * 512 + 129],
                 qphi[:, h_ * TOK + c_ * 128:h_ * TOK + (c_ + 1) * 128],
                 st_sl(state2, h_))
            immt[j] = c["t"]
            if j >= 1:
                p.wait("t", "v", yn_v[j - 1])
                ytp_t[j - 1] = p.tp(pw_tp[1 - pw], ytmpP[1 - pw][:], id128b[:])
            p.wait("v", "t", immt[j])
            p.drain("v")
            p.do("v", lambda h, w=pw, i_=ii, hh=h_, ci=c_: h.scalar_tensor_tensor(
                uP[w][:, 0:129], pA[:, w * 512:w * 512 + 129],
                picol[:, i_:i_ + 1], yint_sl(hh, ci), ALU.mult, ALU.add))
            p.drain("v")
            p.do("v", lambda h, w=pw, i_=ii: h.tensor_scalar(
                denc[:, w * 4:w * 4 + 1], uP[w][:, 128:129],
                pcol[:, i_:i_ + 1], 1e-6, ALU.mult, ALU.add))
            p.drain("v")
            p.do("v", lambda h, w=pw: h.reciprocal(
                denc[:, w * 4 + 1:w * 4 + 2], denc[:, w * 4:w * 4 + 1]))
            p.drain("v")
            p.do("v", lambda h, w=pw, i_=ii: h.tensor_tensor(
                denc[:, w * 4 + 2:w * 4 + 3], denc[:, w * 4 + 1:w * 4 + 2],
                pcol[:, i_:i_ + 1], ALU.mult))
            if j >= 2:
                p.wait("v", "t", ytp_t[j - 2])
            p.drain("v")
            p.do("v", lambda h, w=pw: h.tensor_scalar(
                ytmpP[w][:], uP[w][:, 0:128], denc[:, w * 4 + 2:w * 4 + 3],
                None, ALU.mult), "v")
            yn_v[j] = c["v"]
            if j >= 1:
                jp = j - 1
                hp, cp = order2[jp]
                p.wait("v", "t", ytp_t[jp])
                p.drain("v")
                p.do("v", lambda h, hh=hp, ci=cp, w=1 - pw: h.tensor_copy(
                    yfm[:, hh * TOK + ci * 128:hh * TOK + (ci + 1) * 128],
                    pw_tp[w]), "v")
        jl = len(order2) - 1
        hp, cp = order2[jl]
        p.wait("t", "v", c["v"])
        ytp_t[jl] = p.tp(pw_tp[jl % 2], ytmpP[jl % 2][:], id128b[:])
        p.wait("v", "t", ytp_t[jl])
        p.drain("v")
        p.do("v", lambda h, hh=hp, ci=cp: h.tensor_copy(
            yfm[:, hh * TOK + ci * 128:hh * TOK + (ci + 1) * 128],
            pw_tp[jl % 2]), "v")
        yfm_done = c["v"]
        p.wait("t", "v", yfm_done)
        p.wait("g", "v", yfm_done)
        if "p2" in stages:
            wait_all_dma("y")
            return

        wait_all_dma("y")
        for ot in range(KT):
            wproj, ldw, wpi = wproj_load(WT["wo"][ot])
            for s_, v_ in ldw:
                p.wait("t", s_, v_)
            p.wait("t", "v", c["v"])
            for nh in range(2):
                for k in range(KT):
                    p.mm(wA(nh), wproj[:, k * 128:(k + 1) * 128],
                         yfm[:, k * TOK + nh * 512:k * TOK + (nh + 1) * 512],
                         start=(k == 0), stop=(k == KT - 1))
            omm = c["t"]
            wproj_done(wpi)
            xb = xtile[ot % 2]
            dxs = "dx0" if ot % 2 == 0 else "dx1"
            if ot == 0:
                p.wait("y", "v", c["v"])
            if ot >= 2:
                p.wait("y", "v", c["v"])
            p.dma("y", xb[:], X_SAVE.rearrange("(k p) t -> k p t", p=128)[ot],
                  sem=dxs)
            ldx = c[dxs]
            p.wait("v", "t", omm)
            p.wait("v", dxs, ldx)
            p.wait("v", "dst", c["dst"])
            p.drain("v")
            p.do("v", lambda h, o=ot, b=xb: h.scalar_tensor_tensor(
                tscr[0][:, 0:TOK], pA[:, 0:1024],
                bocol[:, o:o + 1], b[:], ALU.add, ALU.add), "v")
            wo_v = c["v"]
            p.wait("y", "v", wo_v)
            p.dma("y", X_SAVE.rearrange("(k p) t -> k p t", p=128)[ot],
                  tscr[0][:, 0:TOK], sem="dst")
        p.barrier()

        if "ffn2" not in stages:
            wait_all_dma("y")
            return
        ffn(X_SAVE, OUT)

        wait_all_dma("y")

    with nc.Block() as block:
        @block.tensor
        def _(tensor):
            script(P("t", tensor, sems))

        @block.vector
        def _(vector):
            script(P("v", vector, sems))

        @block.scalar
        def _(scalar):
            script(P("s", scalar, sems))

        @block.gpsimd
        def _(gpsimd):
            script(P("g", gpsimd, sems))

        @block.sync
        def _(sync):
            script(P("y", sync, sems))

    es.close()
    return nc


def _host_inputs(inputs, n_cores=8):
    x = np.ascontiguousarray(np.asarray(inputs["L0_attn_x"], dtype=np.float32))

    def tiled(w):
        w = np.asarray(w, np.float32)
        t = w.reshape(KT, 128, KT, 128).transpose(2, 1, 0, 3)
        return np.ascontiguousarray(t)

    wq_t = tiled(inputs["L0_attn_Wq"])
    wk_t = tiled(inputs["L0_attn_Wk"])
    wv_t = tiled(inputs["L0_attn_Wv"])
    wo_t = tiled(inputs["L0_attn_Wo"])
    wg = np.ascontiguousarray(np.asarray(inputs["L0_attn_Wg"], np.float32))
    bg = np.ascontiguousarray(np.asarray(inputs["L0_attn_bg"], np.float32).reshape(H, 1))
    bo = np.ascontiguousarray(
        np.asarray(inputs["L0_attn_bo"], np.float32).reshape(KT, 128).T)
    wgt = np.asarray(inputs["L0_ffn_w_gate"], np.float32).T
    wut = np.asarray(inputs["L0_ffn_w_up"], np.float32).T
    wdt = np.asarray(inputs["L0_ffn_w_down"], np.float32).T
    qg = KT // n_cores
    qd = -(-FT // n_cores)
    d_counts = [FT // n_cores + (1 if i < FT % n_cores else 0)
                for i in range(n_cores)]
    d_start = [sum(d_counts[:i]) for i in range(n_cores)]

    maskt = (np.arange(128)[:, None] <= np.arange(128)[None, :]).astype(np.float32)
    import ml_dtypes
    id128b = np.eye(128, dtype=ml_dtypes.bfloat16)
    id128f = np.eye(128, dtype=np.float32)
    id16f = np.eye(16, dtype=np.float32)
    onescol = np.ones((128, 1), np.float32)
    ones128 = np.ones((128, 128), np.float32)
    constt = np.array([[1e-5, 1.0 / 128.0, 0.0, 0.0]], np.float32)
    onesrow = np.ones((1, 128), np.float32)

    half = 64
    inv = 1.0 / (10000.0 ** (np.arange(half, dtype=np.float64) / half))

    maps = []
    for core in range(n_cores):
        b, hf = core // 2, core % 2
        t0 = hf * TOK
        pos = np.arange(t0, t0 + TOK, dtype=np.float64)
        fr = np.outer(inv, pos)
        cosv = np.cos(fr).astype(np.float32)
        sinv = np.sin(fr).astype(np.float32)
        trigc = np.ascontiguousarray(np.concatenate([cosv, cosv], 0))
        trigs = np.ascontiguousarray(np.concatenate([sinv, sinv], 0))
        x_fm = np.ascontiguousarray(x[b, t0:t0 + TOK, :].T)
        mr = np.full((128, 1), float(hf), np.float32)
        wown_g = np.ascontiguousarray(
            wgt[core * qg * 128:(core + 1) * qg * 128])
        wown_u = np.ascontiguousarray(
            wut[core * qg * 128:(core + 1) * qg * 128])
        wown_d = np.zeros((qd * 128, D), np.float32)
        nd = d_counts[core]
        wown_d[:nd * 128] = wdt[d_start[core] * 128:(d_start[core] + nd) * 128]
        maps.append(dict(
            x_fm=x_fm, wq=wq_t, wk=wk_t, wv=wv_t, wo=wo_t, wg=wg, bg=bg, bo=bo,
            wown_g=wown_g, wown_u=wown_u, wown_d=wown_d,
            trigc=trigc, trigs=trigs, maskt=maskt,
            id128b=id128b, id128f=id128f, id16f=id16f, onescol=onescol,
            ones128=ones128, onesrow=onesrow, constt=constt,
            mrecv=mr))
    return maps


def kernel(**inputs):
    n_cores = 8
    maps = _host_inputs(inputs, n_cores)
    nc = build_program(n_cores)
    res = run_bass_kernel_spmd(nc, maps, core_ids=list(range(n_cores)))
    x = np.asarray(inputs["L0_attn_x"])
    B, L, _ = x.shape
    out = np.empty((B, L, D), np.float32)
    for core in range(n_cores):
        b, hf = core // 2, core % 2
        out[b, hf * TOK:(hf + 1) * TOK, :] = np.asarray(
            res.results[core]["out_fm"]).T
    return out


# version 21
# speedup vs baseline: 1.0126x; 1.0126x over previous
import contextlib

import numpy as np
import concourse.bass as bass
import concourse.mybir as mybir
from concourse.bass_utils import run_bass_kernel_spmd

F32 = mybir.dt.float32
BF16 = mybir.dt.bfloat16
ALU = mybir.AluOpType
ACT = mybir.ActivationFunctionType

MAGIC = float(np.float32(1.5 * 2.0**23))
D = 2048
KT = 16
DFF = 5504
FT = 43
H = 16
TOK = 1024
NCH = 8
FC = 512
NFC = 2


class P:
    def __init__(self, cur, handle, sems):
        self.cur = cur
        self.h = handle
        self.sems = sems
        self.cnt = {name: 0 for name in sems}

    def on(self, eng):
        return eng == self.cur

    def do(self, eng, fn, sem=None, inc=1):
        if self.on(eng):
            instr = fn(self.h)
            if sem is not None:
                instr.then_inc(self.sems[sem], inc)
        if sem is not None:
            self.cnt[sem] += inc
            return self.cnt[sem]
        return None

    def dma(self, eng, out, in_, sem="d"):
        return self.do(eng, lambda h: h.dma_start(out=out, in_=in_), sem, 16)

    def wait(self, eng, sem, val):
        assert val <= self.cnt[sem], f"future wait {sem}:{val}>{self.cnt[sem]}"
        if self.on(eng):
            self.h.wait_ge(self.sems[sem], val)

    def drain(self, eng):
        if self.on(eng):
            self.h.drain()

    def mm(self, out, lhsT, rhs, start=True, stop=True, sem="t"):
        return self.do("t", lambda h: h.matmul(out, lhsT, rhs, start=start, stop=stop), sem)

    def tp(self, out, in_, ident, sem="t"):
        return self.do("t", lambda h: h.transpose(out, in_, ident), sem)

    def barrier(self, engs="tvsg"):
        for e in engs:
            self.drain(e)
            self.do(e, lambda h: h.sem_inc(self.sems["b" + e], 1))
            self.cnt["b" + e] += 1
        for e in engs:
            for e2 in engs:
                if e2 != e:
                    self.wait(e, "b" + e2, self.cnt["b" + e2])


def build_program(n_cores, stages=("ffn1", "attn", "ffn2"), debug=False):
    nc = bass.Bass("TRN2", target_bir_lowering=False)

    X = nc.dram_tensor("x_fm", [D, TOK], F32, kind="ExternalInput").ap()
    OUT = nc.dram_tensor("out_fm", [D, TOK], F32, kind="ExternalOutput").ap()
    WT = {nm: nc.dram_tensor(nm, [KT, 128, KT, 128], F32, kind="ExternalInput").ap()
          for nm in ("wq", "wk", "wv", "wo")}
    WG = nc.dram_tensor("wg", [D, H], F32, kind="ExternalInput").ap()
    BG = nc.dram_tensor("bg", [H, 1], F32, kind="ExternalInput").ap()
    BO = nc.dram_tensor("bo", [128, KT], F32, kind="ExternalInput").ap()
    QG = KT // n_cores
    QD = -(-FT // n_cores)
    d_counts = [FT // n_cores + (1 if i < FT % n_cores else 0)
                for i in range(n_cores)]
    d_slot = []
    for ci in range(n_cores):
        for l in range(d_counts[ci]):
            d_slot.append(ci * QD + l)
    WFFN = {"g": nc.dram_tensor("wown_g", [QG * 128, DFF], F32,
                                kind="ExternalInput").ap(),
            "u": nc.dram_tensor("wown_u", [QG * 128, DFF], F32,
                                kind="ExternalInput").ap(),
            "d": nc.dram_tensor("wown_d", [QD * 128, D], F32,
                                kind="ExternalInput").ap()}
    TRIGC = nc.dram_tensor("trigc", [128, TOK], F32, kind="ExternalInput").ap()
    TRIGS = nc.dram_tensor("trigs", [128, TOK], F32, kind="ExternalInput").ap()
    MASKT = nc.dram_tensor("maskt", [128, 128], F32, kind="ExternalInput").ap()
    ID128B = nc.dram_tensor("id128b", [128, 128], BF16, kind="ExternalInput").ap()
    ID128F = nc.dram_tensor("id128f", [128, 128], F32, kind="ExternalInput").ap()
    ID16F = nc.dram_tensor("id16f", [16, 16], F32, kind="ExternalInput").ap()
    ONESCOL = nc.dram_tensor("onescol", [128, 1], F32, kind="ExternalInput").ap()
    ONES128 = nc.dram_tensor("ones128", [128, 128], F32, kind="ExternalInput").ap()
    ONESROW = nc.dram_tensor("onesrow", [1, 128], F32, kind="ExternalInput").ap()
    MRECV = nc.dram_tensor("mrecv", [128, 1], F32, kind="ExternalInput").ap()
    CONSTT = nc.dram_tensor("constt", [1, 4], F32, kind="ExternalInput").ap()
    CCQ_IN = {t: nc.dram_tensor("ccq_in_" + t, [q * nt * 128, 128], BF16,
                                kind="Internal").ap()
              for t, q, nt in (("g", QG, FT), ("u", QG, FT), ("d", QD, KT))}
    CCQ_OUT = {t: nc.dram_tensor("ccq_out_" + t,
                                 [n_cores * q * nt * 128, 128], BF16,
                                 kind="Internal",
                                 addr_space="Shared" if n_cores > 4 else "Local").ap()
               for t, q, nt in (("g", QG, FT), ("u", QG, FT), ("d", QD, KT))}
    X_SAVE = nc.dram_tensor("x_save", [D, TOK], F32, kind="Internal").ap()
    ROWRS = nc.dram_tensor("rowrs", [2, DFF], F32, kind="Internal").ap()
    RRS2 = nc.dram_tensor("rrs2", [2, DFF], F32, kind="Internal").ap()
    CC_IN = nc.dram_tensor("cc_in", [H * 128, 132], BF16, kind="Internal").ap()
    CC_OUT = nc.dram_tensor("cc_out", [2 * H * 128, 132], BF16, kind="Internal").ap()
    DBG = {}
    if debug:
        for nm in ("qphi0", "kfm0", "vts0", "ascr00", "yint00", "kv00",
                   "st10", "st1f", "yscr00", "denc00", "yfm0"):
            DBG[nm] = nc.dram_tensor("dbg_" + nm, [128, 2112], F32, kind="Internal").ap()
    groups = [[2 * i, 2 * i + 1] for i in range(max(1, n_cores // 2))]

    es = contextlib.ExitStack()

    def sb(name, p_, f, dt=F32):
        return es.enter_context(nc.sbuf_tensor(name, [p_, f], dt))

    trigc = sb("trigc_t", 128, TOK)
    trigs = sb("trigs_t", 128, TOK)
    maskt = sb("maskt_t", 128, 128)
    id128b = sb("id128b_t", 128, 128, BF16)
    id128f = sb("id128f_t", 128, 128)
    id16f = sb("id16f_t", 16, 16)
    onescol = sb("onescol_t", 128, 1)
    ones128 = sb("ones128_t", 128, 128)
    onesrow = sb("onesrow_t", 1, 128)
    mrecv = sb("mrecv_t", 128, 1)
    bgcol = sb("bgcol", H, 1)
    bocol = sb("bocol", 128, KT)
    wgtil = sb("wgtil", 128, KT * H, BF16)
    rows = sb("rows", 1, 2048)
    colsc = sb("colsc", 128, 64)
    colsc2 = sb("colsc2", 128, 64)
    denc = sb("denc", 128, 8)
    constt = sb("constt_t", 1, 4)
    nc.const_aps.register(F32, 1e-5, constt[0:1, 0:1])
    nc.const_aps.register(F32, 1.0 / 128.0, constt[0:1, 1:2])
    prow = sb("prow", 1, 256)
    yscr = sb("yscr", 128, 132, BF16)

    AR1 = sb("AR1", 128, 22016)
    AR2 = sb("AR2", 128, 16640)
    AR3 = sb("AR3", 128, 8704)
    AR1b = AR1[:].bitcast(BF16)
    AR2b = AR2[:].bitcast(BF16)
    AR3b = AR3[:].bitcast(BF16)

    def f1(a, b_):
        return AR1[:, a:b_]

    wslab = [f1(0, 5504), f1(5504, 11008)]
    awslab = f1(11008, 16512)
    uslab = f1(16512, 22016)
    wqslab = [AR2b[:, 0:5504], AR2b[:, 5504:11008]]
    srow_b = [AR3[0:1, 0:5504], AR3[32:33, 0:5504]]

    t1buf = f1(0, 16384)
    hrbuf_b = AR1b
    g1buf = AR2b[:, 0:16384]
    wbG = [AR2b[:, 16384:18432], AR2b[:, 18432:20480]]
    wbU = [AR2b[:, 20480:22528], AR2b[:, 22528:24576]]
    wbD = [AR2b[:, 16384:21888], AR2b[:, 21888:27392]]
    xtile = [AR2[:, 13696:14720], AR2[:, 14720:15744]]
    tscr = [AR3[:, 0:1024], AR3[:, 1024:2048]]
    amaxt = AR3[:, 2048:3072]
    rowbM = AR3[:, 3072:4096]
    rowbR = AR3[:, 4096:5120]
    rowbS = AR3[:, 5120:6144]
    rowbI = AR3[:, 6144:7168]
    rowbF = AR3[:, 7168:8192]
    recipBq = AR2[:, 5504:11008]
    scaleBq = AR2[:, 11008:16512]

    ropes = [f1(0, 1024), f1(1024, 2048)]
    qfmf = f1(2048, 3072)
    kropef = f1(3072, 4096)
    pcol = f1(4096, 4224)
    rpcol = f1(4224, 4352)
    rpscr = f1(4352, 4480)
    slast = f1(4480, 4608)
    kfm = AR1b[:, 9216:11264]
    wproj_bufs = [AR1b[:, 11264:13312], AR1b[:, 40896:42944]]
    state1 = AR1b[:, 13312:15424]
    staterecv = AR1b[:, 15424:17536]
    state2 = AR1b[:, 17536:19648]
    beta = AR1[0:16, 9824:10848]
    pbuf = AR1[0:16, 10848:11872]
    z16 = AR1[0:16, 11872:12000]
    yint = AR1b[:, 24000:40896]
    xh = AR2b[:, 0:16384]
    kvbuf = AR2b[:, 16384:33280]
    qphi = AR3b[:, 0:16384]

    pA = es.enter_context(nc.psum_tensor("pA", [128, 2048], F32))
    pB = es.enter_context(nc.psum_tensor("pB", [128, 2048], F32))

    def wA(i, w=512, p0=0, p1=128):
        return pA[p0:p1, i * 512:i * 512 + w]

    def wB(i, w=512, p0=0, p1=128):
        return pB[p0:p1, i * 512:i * 512 + w]

    pB_tpv = pB[:, 1024:1088].bitcast(BF16)
    pB_tpy = pB[:, 512:576].bitcast(BF16)

    sems = {n: es.enter_context(nc.semaphore("sem_" + n))
            for n in (["d", "g2d", "cc", "t", "v", "s", "g",
                       "dx0", "dx1"] + ["dw%d" % i for i in range(8)] +
                      ["dsa", "dsb", "dqr", "dbc", "dcc", "dst"] +
                      ["g2d%s" % ch for ch in "abcdefgh"] +
                      ["b" + e for e in "tvsg"])}
    YDMA_SEMS = (["d", "dx0", "dx1"] + ["dw%d" % i for i in range(8)] +
                 ["dsa", "dsb", "dbc", "dcc", "dst"])

    def st_sl(buf, h_):
        return buf[:, h_ * 132:h_ * 132 + 129]

    def kv_sl(h_, c_):
        return kvbuf[:, (h_ * NCH + c_) * 132:(h_ * NCH + c_) * 132 + 129]

    def yint_sl(h_, c_):
        return yint[:, (h_ * NCH + c_) * 132:(h_ * NCH + c_) * 132 + 129]

    def idx(c_, h_):
        return c_ * 16 + h_

    def script(p: P):
        c = p.cnt

        def dump(nm, ap, eng="v"):
            if not debug:
                return
            p.drain(eng)
            p.do(eng, lambda h: h.sem_inc(p.sems[eng], 1))
            c[eng] += 1
            p.wait("y", eng, c[eng])
            n = ap.shape[-1]
            np_ = ap.shape[0]
            p.dma("y", DBG[nm][0:np_, 0:n], ap, sem="d")
            p.wait("y", "d", c["d"])
            p.wait(eng, "d", c["d"])

        def wait_all_dma(eng):
            for s_ in YDMA_SEMS:
                p.wait(eng, s_, c[s_])
            p.wait(eng, "g2d", c["g2d"])
            for ch_ in "abcdefgh":
                p.wait(eng, "g2d" + ch_, c["g2d" + ch_])
            p.wait(eng, "dqr", c["dqr"])

        def recip_acc(out_ap, in_ap, scr_ap, sem=None):
            return p.do("v", lambda h: h.reciprocal(out_ap, in_ap), sem)

        for dst, src in ((trigc, TRIGC), (trigs, TRIGS), (maskt, MASKT),
                         (id128b, ID128B), (id128f, ID128F), (id16f, ID16F),
                         (onescol, ONESCOL), (ones128, ONES128),
                         (constt, CONSTT),
                         (onesrow, ONESROW), (mrecv, MRECV), (bgcol, BG),
                         (bocol, BO)):
            p.dma("y", dst[:], src)
        p.do("g", lambda h: h.dma_start(
            out=wgtil[:].rearrange("p (k hh) -> p k hh", k=KT),
            in_=WG.rearrange("(k p) hh -> p k hh", p=128)), "g2d", 16)
        ld0, g2d0 = c["d"], c["g2d"]
        for e in "tvsg":
            p.wait(e, "d", ld0)
            p.wait(e, "g2d", g2d0)
        p.wait("y", "g2d", g2d0)

        def quant(tag, n_groups, n_out):
            WSRC = WFFN[tag]
            nspan = (n_out + 2047) // 2048
            ncc = n_out // 128
            CIN4 = CCQ_IN[tag].rearrange("(l ot p) m -> l ot p m",
                                         ot=ncc, p=128)
            vdone, wqd, chain_d, scd, lds, absd = {}, {}, {}, {}, {}, {}
            span_rd, span_bc, srow_s = {}, {}, {}
            win_ev = {0: 0, 1: 0, 2: 0, 3: 0}
            q4 = [n_out // 4 * i for i in range(4)] + [n_out]

            def sA_load(g):
                buf = g % 2
                dss = "dsa" if buf == 0 else "dsb"
                if g >= 2:
                    p.wait("y", "v", vdone[g - 2])
                    p.wait("y", dss, wqd.get(g - 2, 0))
                src_ = WSRC.rearrange("(g p) o -> g p o", p=128)[g]
                for i in range(4):
                    p.dma("y", wslab[buf][:, q4[i]:q4[i + 1]],
                          src_[:, q4[i]:q4[i + 1]], sem="dw%d" % (4 * buf + i))
                lds[g] = [("dw%d" % (4 * buf + i), c["dw%d" % (4 * buf + i)])
                          for i in range(4)]

            def sB_bcast(g):
                buf = g % 2
                p.wait("s", "dqr", chain_d[g])
                for sp in range(nspan):
                    o0 = sp * 2048
                    wsp = min(2048, n_out - o0)
                    p.wait("s", "v", span_rd.get((g - 1, sp), 0))
                    p.wait("s", "dbc", c["dbc"])
                    p.do("s", lambda h, b=buf, o_=o0, w_=wsp: h.dma_start(
                        out=recipBq[:, o_:o_ + w_],
                        in_=ROWRS[b:b + 1, o_:o_ + w_].partition_broadcast(128)[:, 0, :]),
                        "dbc", 16)
                    p.do("s", lambda h, b=buf, o_=o0, w_=wsp: h.dma_start(
                        out=scaleBq[:, o_:o_ + w_],
                        in_=RRS2[b:b + 1, o_:o_ + w_].partition_broadcast(128)[:, 0, :]),
                        "dbc", 16)
                    span_bc[(g, sp)] = c["dbc"]

            def sB_spans(g):
                buf = g % 2
                dss = "dsa" if buf == 0 else "dsb"
                for sp in range(nspan):
                    o0 = sp * 2048
                    wsp = min(2048, n_out - o0)
                    if sp == 0 and g >= 2:
                        p.wait("v", dss, wqd[g - 2])
                    p.wait("v", "dbc", span_bc[(g, sp)])
                    if sp == 0:
                        for s_, v_ in lds[g]:
                            p.wait("v", s_, v_)
                    p.drain("v")
                    p.do("v", lambda h, b=buf, o_=o0, w_=wsp: h.tensor_tensor(
                        uslab[:, o_:o_ + w_],
                        wslab[b][:, o_:o_ + w_], recipBq[:, o_:o_ + w_],
                        ALU.mult))
                    p.drain("v")
                    p.do("v", lambda h, o_=o0, w_=wsp: h.tensor_scalar(
                        uslab[:, o_:o_ + w_], uslab[:, o_:o_ + w_],
                        MAGIC, MAGIC, ALU.add, ALU.subtract))
                    p.drain("v")
                    p.do("v", lambda h, o_=o0, w_=wsp: h.tensor_scalar(
                        uslab[:, o_:o_ + w_], uslab[:, o_:o_ + w_],
                        -1.0, 1.0, ALU.max, ALU.min))
                    p.drain("v")
                    p.do("v", lambda h, b=buf, o_=o0, w_=wsp: h.tensor_tensor(
                        wqslab[b][:, o_:o_ + w_], uslab[:, o_:o_ + w_],
                        scaleBq[:, o_:o_ + w_], ALU.mult), "v")
                    span_rd[(g, sp)] = c["v"]
                vdone[g] = c["v"]

            def sA_abs(g):
                buf = g % 2
                for s_, v_ in lds[g]:
                    p.wait("s", s_, v_)
                p.wait("s", "t", scd.get(g - 1, 0))
                p.drain("s")
                p.do("s", lambda h, b=buf, no=n_out: h.activation(
                    awslab[:, 0:no], wslab[b][:, 0:no], ACT.Abs), "s")
                absd[g] = c["s"]

            def sA_scales(g):
                buf = g % 2
                srow = srow_b[buf]
                p.wait("t", "s", absd[g])
                for oc in range((n_out + 511) // 512):
                    cw = min(512, n_out - oc * 512)
                    p.wait("t", "s", win_ev[oc % 4])
                    p.mm(wA(oc % 4, cw, p1=1), onescol[:],
                         awslab[:, oc * 512:oc * 512 + cw])
                    tmm = c["t"]
                    p.wait("s", "t", tmm)
                    p.drain("s")
                    p.do("s", lambda h, o_=oc, w0=cw, sr=srow: h.activation(
                        sr[0:1, o_ * 512:o_ * 512 + w0], wA(o_ % 4, w0, p1=1),
                        ACT.Copy, bias=1e-5, scale=1.0 / 128.0), "s")
                    win_ev[oc % 4] = c["s"]
                scd[g] = c["t"]
                srow_s[g] = c["s"]

            def sA_chain(g):
                buf = g % 2
                srow = srow_b[buf]
                p.wait("g", "s", srow_s[g])
                p.wait("g", "dqr", c["dqr"])
                p.do("g", lambda h, no=n_out, b=buf, sr=srow: h.dma_start(
                    out=ROWRS[b:b + 1, 0:no], in_=sr[0:1, 0:no]), "dqr", 16)
                p.do("g", lambda h, no=n_out, b=buf, sr=srow: h.dma_start(
                    out=RRS2[b:b + 1, 0:no], in_=sr[0:1, 0:no]), "dqr", 16)
                p.wait("g", "dqr", c["dqr"])
                cbuf = colsc if buf == 0 else colsc2
                p.do("g", lambda h, no=n_out, n=ncc, b=buf, cb=cbuf: h.dma_start(
                    out=cb[:, 0:n],
                    in_=ROWRS[b:b + 1, 0:no].rearrange(
                        "one (pp f) -> (one pp) f", pp=128)), "dqr", 16)
                d2 = c["dqr"]
                p.wait("v", "dqr", d2)
                p.drain("v")
                p.do("v", lambda h, n=ncc, cb=cbuf: h.reciprocal(
                    cb[:, 0:n], cb[:, 0:n]), "v")
                rec_v = c["v"]
                p.wait("g", "v", rec_v)
                p.do("g", lambda h, no=n_out, n=ncc, b=buf, cb=cbuf: h.dma_start(
                    out=ROWRS[b:b + 1, 0:no].rearrange(
                        "one (pp f) -> (one pp) f", pp=128),
                    in_=cb[:, 0:n]), "dqr", 16)
                chain_d[g] = c["dqr"]

            def sB_store(g):
                buf = g % 2
                dss = "dsa" if buf == 0 else "dsb"
                p.wait("y", "v", vdone[g])
                for ot in range(ncc):
                    p.dma("y", CIN4[g, ot],
                          wqslab[buf][:, ot * 128:(ot + 1) * 128], sem=dss)
                wqd[g] = c[dss]

            for it in range(n_groups + 1):
                if it < n_groups:
                    sA_load(it)
                if it >= 1:
                    sB_bcast(it - 1)
                    sB_spans(it - 1)
                if it < n_groups:
                    sA_abs(it)
                    sA_scales(it)
                    sA_chain(it)
                if it >= 1:
                    sB_store(it - 1)
            p.wait("y", "dsa", c["dsa"])
            p.wait("y", "dsb", c["dsb"])
            p.wait("y", "v", vdone[n_groups - 1])

        quant("g", QG, DFF)
        quant("u", QG, DFF)
        quant("d", QD, D)
        allg = [[i for i in range(n_cores)]]
        ccq_done = {}
        for t in ("g", "u", "d"):
            p.wait("g", "dsa", c["dsa"])
            p.wait("g", "dsb", c["dsb"])
            p.do("g", lambda h, t_=t: h.collective_compute(
                "AllGather", ALU.bypass, ins=[CCQ_IN[t_]], outs=[CCQ_OUT[t_]],
                replica_groups=allg), "cc", 1)
            ccq_done[t] = c["cc"]
        p.barrier()

        def ln_stats(XSRC):
            wait_all_dma("y")
            for k in range(KT):
                xb = xtile[k % 2]
                dxs = "dx0" if k % 2 == 0 else "dx1"
                if k == 0:
                    p.wait("y", "t", c["t"])
                    p.wait("y", "v", c["v"])
                    p.wait("y", "s", c["s"])
                if k >= 2:
                    p.wait("y", "t", c["t"])
                p.dma("y", xb[:], XSRC.rearrange("(k p) t -> k p t", p=128)[k],
                      sem=dxs)
                ldx = c[dxs]
                sq = tscr[k % 2]
                p.wait("s", dxs, ldx)
                if k >= 2:
                    p.wait("s", "t", c["t"])
                p.drain("s")
                sk = p.do("s", lambda h, b=xb, q=sq: h.activation(
                    q[:, 0:TOK], b[:], ACT.Square), "s")
                p.wait("t", dxs, ldx)
                p.wait("t", "s", sk)
                if k == 0:
                    p.wait("t", "v", c["v"])
                first, last = (k == 0), (k == KT - 1)
                for nh in range(2):
                    p.mm(wA(0 + nh, p1=1), onescol[:],
                         xb[:, nh * 512:(nh + 1) * 512], start=first, stop=last)
                    p.mm(wA(2 + nh, p1=1), onescol[:],
                         sq[:, nh * 512:(nh + 1) * 512], start=first, stop=last)
            tdone = c["t"]
            p.wait("v", "t", tdone)
            p.wait("v", "dst", c["dst"])
            p.drain("v")
            for nh in range(2):
                p.do("v", lambda h, n=nh: h.tensor_scalar(
                    rows[0:1, n * 512:(n + 1) * 512], wA(n, p1=1),
                    1.0 / D, None, ALU.mult))
                p.do("v", lambda h, n=nh: h.tensor_scalar(
                    tscr[0][0:1, n * 512:(n + 1) * 512], wA(2 + n, p1=1),
                    1.0 / D, None, ALU.mult))
            p.drain("v")
            p.do("v", lambda h: h.tensor_tensor(
                tscr[1][0:1, 0:TOK], rows[0:1, 0:TOK], rows[0:1, 0:TOK], ALU.mult))
            p.drain("v")
            p.do("v", lambda h: h.tensor_tensor(
                tscr[0][0:1, 0:TOK], tscr[0][0:1, 0:TOK], tscr[1][0:1, 0:TOK],
                ALU.subtract), "v")
            p.drain("v")
            p.do("v", lambda h: h.tensor_scalar(
                tscr[0][0:1, 0:TOK], tscr[0][0:1, 0:TOK], 1e-5, None, ALU.add), "v")
            vv = c["v"]
            p.wait("s", "v", vv)
            p.drain("s")
            p.do("s", lambda h: h.activation(
                tscr[0][0:1, 0:TOK], tscr[0][0:1, 0:TOK], ACT.Ln), "s")
            sln = c["s"]
            p.wait("v", "s", sln)
            p.drain("v")
            p.do("v", lambda h: h.tensor_scalar(
                tscr[0][0:1, 0:TOK], tscr[0][0:1, 0:TOK], -0.5, None, ALU.mult), "v")
            vln = c["v"]
            p.wait("s", "v", vln)
            p.drain("s")
            p.do("s", lambda h: h.activation(
                rows[0:1, 1024:2048], tscr[0][0:1, 0:TOK], ACT.Exp), "s")
            sR = c["s"]
            p.wait("v", "s", sR)
            p.wait("t", "s", sR)

        def bcast(row_ap, win, dst):
            n = row_ap.shape[-1]
            p.wait("y", "v", c["v"])
            p.wait("y", "s", c["s"])
            p.wait("y", "dbc", c["dbc"])
            p.dma("y", RRS2[0:1, 0:n], row_ap, sem="dbc")
            p.wait("y", "dbc", c["dbc"])
            bc = p.dma("y", dst, RRS2[0:1, 0:n].partition_broadcast(128)[:, 0, :],
                       sem="dbc")
            return bc

        def amax_bc(dst_rowb, mul_rowb, mul_const):
            p.wait("t", "v", c["v"])
            for b_ in range(8):
                p.tp(pA[:, b_ * 128:(b_ + 1) * 128],
                     amaxt[:, b_ * 128:(b_ + 1) * 128], id128f[:])
            tpd = c["t"]
            p.wait("v", "t", tpd)
            p.drain("v")
            p.do("v", lambda h: h.tensor_copy(tscr[1][:, 0:TOK], pA[:, 0:TOK]))
            p.drain("v")
            p.do("v", lambda h: h.tensor_reduce(
                colsc[:, 0:8],
                tscr[1][:, 0:TOK].rearrange("p (b f) -> p b f", f=128),
                axis=mybir.AxisListType.X, op=ALU.max,
                apply_absolute_value=True))
            p.drain("v")
            p.do("v", lambda h: h.tensor_scalar(
                colsc[:, 0:8], colsc[:, 0:8], 1e-5, None, ALU.max))
            p.drain("v")
            recip_acc(colsc[:, 8:16], colsc[:, 0:8], colsc[:, 24:32])
            p.drain("v")
            p.do("v", lambda h: h.tensor_scalar(
                colsc[:, 16:24], colsc[:, 8:16], 127.0, None, ALU.mult), "v")
            for b_ in range(8):
                p.drain("v")
                p.do("v", lambda h, bb=b_: h.tensor_scalar(
                    tscr[1][:, 0:128], id128f[:], colsc[:, 16 + bb:17 + bb], None,
                    ALU.mult))
                p.drain("v")
                p.do("v", lambda h, bb=b_: h.tensor_scalar(
                    tscr[1][:, 128:256], id128f[:], colsc[:, 0 + bb:1 + bb], None,
                    ALU.mult), "v")
                dgv = c["v"]
                p.wait("t", "v", dgv)
                p.mm(pA[:, 1024 + b_ * 128:1152 + b_ * 128], ones128[:],
                     tscr[1][:, 0:128])
                tmm = p.mm(pB[:, b_ * 128:(b_ + 1) * 128], ones128[:],
                           tscr[1][:, 128:256])
                p.wait("v", "t", tmm)
            p.drain("v")
            p.do("v", lambda h: h.tensor_copy(rowbS[:, 0:TOK], pA[:, 1024:2048]))
            p.drain("v")
            p.do("v", lambda h: h.scalar_tensor_tensor(
                dst_rowb[:, 0:TOK], pB[:, 0:TOK], mul_const,
                mul_rowb[:, 0:TOK], ALU.mult, ALU.mult), "v")

        def ffn(XSRC, XDST):
            XS3 = XSRC.rearrange("(k p) t -> k p t", p=128)
            XD3 = XDST.rearrange("(k p) t -> k p t", p=128)
            gg4 = CCQ_OUT["g"].rearrange("(s ot p) m -> s ot p m", ot=FT, p=128)
            uu4 = CCQ_OUT["u"].rearrange("(s ot p) m -> s ot p m", ot=FT, p=128)
            dd4 = CCQ_OUT["d"].rearrange("(s ot p) m -> s ot p m", ot=KT, p=128)
            ln_stats(XSRC)
            bcast(rows[0:1, 0:TOK], None, rowbM[:, 0:TOK])
            bcR = bcast(rows[0:1, 1024:2048], None, rowbR[:, 0:TOK])
            p.wait("v", "dbc", bcR)
            t1v, absS, fold = {}, {}, {}

            def emit_fold(k):
                p.wait("v", "s", absS[k])
                p.drain("v")
                q = tscr[k % 2]
                if k == 0:
                    p.do("v", lambda h, q_=q: h.tensor_copy(
                        amaxt[:, 0:TOK], q_[:, 0:TOK]), "v")
                else:
                    p.do("v", lambda h, q_=q: h.tensor_tensor(
                        amaxt[:, 0:TOK], amaxt[:, 0:TOK], q_[:, 0:TOK],
                        ALU.max), "v")
                fold[k] = c["v"]

            for k in range(KT):
                xb = xtile[k % 2]
                dxs = "dx0" if k % 2 == 0 else "dx1"
                if k == 0:
                    p.wait("y", "v", c["v"])
                    p.wait("y", "s", c["s"])
                if k >= 2:
                    p.wait("y", "v", t1v[k - 2])
                p.dma("y", xb[:], XS3[k], sem=dxs)
                ldx = c[dxs]
                p.wait("v", dxs, ldx)
                p.drain("v")
                p.do("v", lambda h, b=xb, kk=k: h.scalar_tensor_tensor(
                    t1buf[:, kk * TOK:(kk + 1) * TOK], b[:], 0.0,
                    rowbM[:, 0:TOK], ALU.add, ALU.subtract), "v")
                t1v[k] = c["v"]
                p.wait("s", "v", t1v[k])
                if k >= 2:
                    p.wait("s", "v", fold[k - 2])
                p.drain("s")
                absS[k] = p.do("s", lambda h, q=tscr[k % 2], kk=k: h.activation(
                    q[:, 0:TOK], t1buf[:, kk * TOK:(kk + 1) * TOK], ACT.Abs),
                    "s")
                if k >= 1:
                    emit_fold(k - 1)
            emit_fold(KT - 1)
            amax_bc(rowbI, rowbR, 1.0 / 127.0)
            g1v = {}
            for k in range(KT):
                p.drain("v")
                p.do("v", lambda h, kk=k: h.tensor_tensor(
                    tscr[0][:, 0:TOK], t1buf[:, kk * TOK:(kk + 1) * TOK],
                    rowbS[:, 0:TOK], ALU.mult))
                p.drain("v")
                p.do("v", lambda h, kk=k: h.tensor_scalar(
                    g1buf[:, kk * TOK:(kk + 1) * TOK], tscr[0][:, 0:TOK],
                    MAGIC, MAGIC, ALU.add, ALU.subtract), "v")
                g1v[k] = c["v"]

            gld, gumm, epd, hrv, absS2, fold2 = {}, {}, {}, {}, {}, {}

            def gu_load(og):
                pg = og % 2
                if og == 0:
                    p.wait("y", "cc", ccq_done["u"])
                if og >= 2:
                    p.wait("y", "t", gumm[og - 2])
                for i, (buf, W4) in enumerate(((wbG[pg], gg4), (wbU[pg], uu4))):
                    for hf in range(2):
                        s_ = "dw%d" % (pg * 4 + i * 2 + hf)
                        for g in range(hf * 8, (hf + 1) * 8):
                            p.dma("y", buf[:, g * 128:(g + 1) * 128],
                                  W4[g, og], sem=s_)
                gld[og] = [("dw%d" % (pg * 4 + j), c["dw%d" % (pg * 4 + j)])
                           for j in range(4)]

            def emit_fold2(og):
                p.wait("s", "v", c["v"])
                if og >= 1:
                    p.wait("s", "v", fold2.get(og - 1, 0))
                p.drain("s")
                absS2[og] = p.do("s", lambda h, o=og: h.activation(
                    tscr[1][:, 0:TOK],
                    hrbuf_b[:, o * TOK:(o + 1) * TOK], ACT.Abs), "s")
                p.wait("v", "s", absS2[og])
                p.drain("v")
                if og == 0:
                    p.do("v", lambda h: h.tensor_copy(
                        amaxt[:, 0:TOK], tscr[1][:, 0:TOK]), "v")
                else:
                    p.do("v", lambda h: h.tensor_tensor(
                        amaxt[:, 0:TOK], amaxt[:, 0:TOK], tscr[1][:, 0:TOK],
                        ALU.max), "v")
                fold2[og] = c["v"]

            gu_load(0)
            for og in range(FT):
                if og + 1 < FT:
                    gu_load(og + 1)
                pg = og % 2
                for s_, v_ in gld[og]:
                    p.wait("t", s_, v_)
                if og >= 2:
                    p.wait("t", "v", epd[og - 2])
                for g in range(KT):
                    if og == 0:
                        p.wait("t", "v", g1v[g])
                    for ch in range(2):
                        p.mm(pA[:, pg * 1024 + ch * 512:pg * 1024 + ch * 512 + 512],
                             wbG[pg][:, g * 128:(g + 1) * 128],
                             g1buf[:, g * TOK + ch * 512:g * TOK + ch * 512 + 512],
                             start=(g == 0), stop=(g == KT - 1))
                    for ch in range(2):
                        p.mm(pB[:, pg * 1024 + ch * 512:pg * 1024 + ch * 512 + 512],
                             wbU[pg][:, g * 128:(g + 1) * 128],
                             g1buf[:, g * TOK + ch * 512:g * TOK + ch * 512 + 512],
                             start=(g == 0), stop=(g == KT - 1))
                gumm[og] = c["t"]
                p.wait("v", "t", gumm[og])
                p.wait("v", "s", c["s"])
                p.drain("v")
                p.do("v", lambda h, g_=pg: h.scalar_tensor_tensor(
                    tscr[0][:, 0:TOK], pA[:, g_ * 1024:(g_ + 1) * 1024], 0.0,
                    rowbI[:, 0:TOK], ALU.add, ALU.mult), "v")
                gtv = c["v"]
                p.wait("s", "v", gtv)
                if og >= 1:
                    p.wait("s", "v", fold2.get(og - 1, 0))
                p.drain("s")
                sil = p.do("s", lambda h: h.activation(
                    tscr[1][:, 0:TOK], tscr[0][:, 0:TOK], ACT.Silu), "s")
                p.wait("v", "s", sil)
                p.drain("v")
                p.do("v", lambda h, g_=pg, o=og: h.scalar_tensor_tensor(
                    hrbuf_b[:, o * TOK:(o + 1) * TOK],
                    pB[:, g_ * 1024:(g_ + 1) * 1024], 0.0,
                    tscr[1][:, 0:TOK], ALU.add, ALU.mult), "v")
                hrv[og] = c["v"]
                epd[og] = c["v"]
                if og >= 1:
                    emit_fold2(og - 1)
            emit_fold2(FT - 1)
            amax_bc(rowbF, rowbI, 0.5 / 127.0)
            g2v = {}
            for f in range(FT):
                p.drain("v")
                p.do("v", lambda h, ff=f: h.tensor_tensor(
                    tscr[0][:, 0:TOK], hrbuf_b[:, ff * TOK:(ff + 1) * TOK],
                    rowbS[:, 0:TOK], ALU.mult))
                p.drain("v")
                p.do("v", lambda h, ff=f: h.tensor_scalar(
                    hrbuf_b[:, ff * TOK:(ff + 1) * TOK], tscr[0][:, 0:TOK],
                    MAGIC, MAGIC, ALU.add, ALU.subtract), "v")
                g2v[f] = c["v"]

            dld, dmm, ep2, dstc = {}, {}, {}, {}
            qr = [0, 6, 11, 17, 22, 28, 33, 38, FT]

            def d_load(og):
                pg = og % 2
                if og < 2:
                    p.wait("y", "t", gumm[FT - 1])
                else:
                    p.wait("y", "t", dmm[og - 2])
                if og == 0:
                    p.wait("y", "cc", ccq_done["d"])
                for i in range(8):
                    s_ = "dw%d" % (pg * 4 + i % 4)
                    for g in range(qr[i], qr[i + 1]):
                        p.dma("y", wbD[pg][:, g * 128:(g + 1) * 128],
                              dd4[d_slot[g], og], sem=s_)
                dld[og] = [("dw%d" % (pg * 4 + j), c["dw%d" % (pg * 4 + j)])
                           for j in range(4)]

            d_load(0)
            for og in range(KT):
                if og + 1 < KT:
                    d_load(og + 1)
                pg = og % 2
                for s_, v_ in dld[og]:
                    p.wait("t", s_, v_)
                if og >= 2:
                    p.wait("t", "v", ep2[og - 2])
                for g in range(FT):
                    if og == 0:
                        p.wait("t", "v", g2v[g])
                    for ch in range(2):
                        p.mm(pA[:, pg * 1024 + ch * 512:pg * 1024 + ch * 512 + 512],
                             wbD[pg][:, g * 128:(g + 1) * 128],
                             hrbuf_b[:, g * TOK + ch * 512:g * TOK + ch * 512 + 512],
                             start=(g == 0), stop=(g == FT - 1))
                dmm[og] = c["t"]
                xb = xtile[og % 2]
                dxs = "dx0" if og % 2 == 0 else "dx1"
                if og == 0:
                    p.wait("y", "v", c["v"])
                if og >= 2:
                    p.wait("y", "v", ep2[og - 2])
                    p.wait("y", "dst" if og % 2 == 0 else "d", dstc[og - 2])
                p.dma("y", xb[:], XS3[og], sem=dxs)
                ldx = c[dxs]
                p.wait("v", "t", dmm[og])
                p.wait("v", dxs, ldx)
                p.drain("v")
                p.do("v", lambda h, g_=pg: h.scalar_tensor_tensor(
                    tscr[0][:, 0:TOK], pA[:, g_ * 1024:(g_ + 1) * 1024], 0.0,
                    rowbF[:, 0:TOK], ALU.add, ALU.mult))
                p.drain("v")
                p.do("v", lambda h, b=xb: h.tensor_tensor(
                    b[:], b[:], tscr[0][:, 0:TOK], ALU.add), "v")
                ep2[og] = c["v"]
                p.wait("y", "v", ep2[og])
                sst = "dst" if og % 2 == 0 else "d"
                p.dma("y", XD3[og], xb[:], sem=sst)
                dstc[og] = c[sst]
            p.barrier()

        if "ffn1" in stages:
            ffn(X, X_SAVE)
        if "attn" not in stages:
            wait_all_dma("y")
            return

        ln_stats(X_SAVE)
        bcast(rows[0:1, 0:TOK], None, rowbM[:, 0:TOK])
        bc4 = bcast(rows[0:1, 1024:2048], None, rowbR[:, 0:TOK])
        p.wait("v", "dbc", bc4)
        for k in range(KT):
            xb = xtile[k % 2]
            dxs = "dx0" if k % 2 == 0 else "dx1"
            if k == 0:
                p.wait("y", "t", c["t"])
                p.wait("y", "v", c["v"])
                p.wait("v", "dst", c["dst"])
            if k >= 2:
                p.wait("y", "v", c["v"])
            p.dma("y", xb[:], X_SAVE.rearrange("(k p) t -> k p t", p=128)[k],
                  sem=dxs)
            ldx = c[dxs]
            p.wait("v", dxs, ldx)
            p.drain("v")
            p.do("v", lambda h, b=xb: h.scalar_tensor_tensor(
                tscr[0][:, 0:TOK], b[:], 0.0, rowbM[:, 0:TOK],
                ALU.add, ALU.subtract))
            p.drain("v")
            p.do("v", lambda h, kk=k: h.tensor_tensor(
                xh[:, kk * TOK:(kk + 1) * TOK], tscr[0][:, 0:TOK],
                rowbR[:, 0:TOK], ALU.mult), "v")
        xh_done = c["v"]
        p.wait("t", "v", xh_done)
        p.wait("g", "v", xh_done)

        for nh in range(2):
            for k in range(KT):
                p.mm(wA(nh, p1=16), wgtil[:, k * 16:(k + 1) * 16],
                     xh[:, k * TOK + nh * 512:k * TOK + (nh + 1) * 512],
                     start=(k == 0), stop=(k == KT - 1))
        bmm = c["t"]
        p.wait("s", "t", bmm)
        p.drain("s")
        for nh in range(2):
            p.do("s", lambda h, n=nh: h.activation(
                beta[:, n * 512:(n + 1) * 512], wA(n, p1=16), ACT.Sigmoid,
                bias=bgcol[:], scale=1.0), "s")
        sbeta = c["s"]
        p.wait("v", "s", sbeta)
        p.drain("v")
        p.do("v", lambda h: h.tensor_scalar(
            beta[:, 0:TOK], beta[:, 0:TOK], 0.8, 0.9995, ALU.max, ALU.min))
        p.do("v", lambda h: h.memset(z16[:, 0:128], 0.0))
        p.drain("v")
        for c_ in range(NCH):
            p.do("v", lambda h, cc=c_: h.tensor_tensor_scan(
                pbuf[:, cc * 128:(cc + 1) * 128], beta[:, cc * 128:(cc + 1) * 128],
                z16[:, 0:128], 1.0, ALU.mult, ALU.add), "v")
        pscan = c["v"]
        p.wait("t", "v", pscan)
        for c_ in range(NCH):
            p.tp(pA[:, 1024 + c_ * 16:1024 + (c_ + 1) * 16],
                 pbuf[:, c_ * 128:(c_ + 1) * 128], id16f[:])
        ptp = c["t"]
        p.wait("v", "t", ptp)
        p.drain("v")
        p.do("v", lambda h: h.tensor_copy(pcol[:], pA[:, 1024:1152]), "v")
        p.drain("v")
        recip_acc(rpcol[:], pcol[:], rpscr[:], "v")
        p.do("v", lambda h: h.sem_inc(p.sems["v"], 1))
        c["v"] += 1
        p.wait("g", "v", c["v"])
        p.do("g", lambda h: h.dma_start(
            out=prow[0:1, 0:128], in_=pcol[127:128, :]), "dqr", 16)
        vsl = c["dqr"]
        p.wait("t", "dqr", vsl)
        tm = p.mm(pA[:, 1536:1664], onesrow[:], prow[0:1, 0:128])
        p.wait("v", "t", tm)
        p.wait("v", "dqr", vsl)
        p.drain("v")
        p.do("v", lambda h: h.tensor_copy(slast[:], pA[:, 1536:1664]))
        p.do("v", lambda h: h.memset(state1[:, 0:H * 132], 0.0))
        p.drain("v")
        p.do("v", lambda h: h.memset(prow[0:1, 128:144], 1.0))
        p.drain("v")
        for c_ in range(1, NCH):
            p.do("v", lambda h, cc=c_: h.tensor_tensor(
                prow[0:1, 128 + cc * 16:144 + cc * 16],
                prow[0:1, 112 + cc * 16:128 + cc * 16],
                prow[0:1, cc * 16 - 16:cc * 16], ALU.mult))
            p.drain("v")
        p.do("v", lambda h: h.sem_inc(p.sems["v"], 1))
        c["v"] += 1
        initv = c["v"]
        p.wait("t", "v", initv)
        p.wait("g", "v", initv)

        wp_idx = [0]
        wp_tmm = {}

        def wproj_load(src_ap):
            i_ = wp_idx[0]
            buf = wproj_bufs[i_ % 2]
            if i_ >= 2:
                p.wait("g", "t", wp_tmm[i_ - 2])
            sems_ = []
            for q in range(4):
                s_ = "g2d%s" % ("abcdefgh"[(i_ % 2) * 4 + q])
                p.do("g", lambda h, q_=q: h.dma_start(
                    out=buf[:, q_ * 512:(q_ + 1) * 512].rearrange(
                        "p (k n) -> p k n", k=4),
                    in_=src_ap[:, q_ * 4:(q_ + 1) * 4, :]), s_, 16)
                sems_.append((s_, c[s_]))
            wp_idx[0] += 1
            return buf, sems_, i_

        def wproj_done(i_):
            wp_tmm[i_] = c["t"]

        pw_S = [pB[:, 0:128], pB[:, 512:640]]
        pw_Y = [pB[:, 1024:1153], pB[:, 1536:1665]]
        pw_T = [pA[:, 0:64].bitcast(BF16), pA[:, 1536:1600].bitcast(BF16)]
        pw_K = [pA[:, 512:641], pA[:, 1024:1153]]
        ascrP = [AR3b[:, 16384:16512], AR3b[:, 16512:16640]]
        ksbP = [AR3b[:, 16640:16768], AR3b[:, 16768:16896]]
        fscr = qfmf[:, 0:129]

        def rope_phi(src_psum, dst, sign_h):
            p.drain("v")
            p.do("v", lambda h: h.tensor_tensor(
                ropes[0][0:64, 0:TOK], src_psum[0:64, :], trigc[0:64, :],
                ALU.mult))
            p.do("v", lambda h: h.tensor_tensor(
                ropes[1][0:64, 0:TOK], src_psum[64:128, :], trigs[64:128, :],
                ALU.mult))
            p.do("v", lambda h: h.tensor_tensor(
                ropes[0][64:128, 0:TOK], src_psum[0:64, :], trigs[0:64, :],
                ALU.mult))
            p.do("v", lambda h: h.tensor_tensor(
                ropes[1][64:128, 0:TOK], src_psum[64:128, :], trigc[64:128, :],
                ALU.mult), "v")
            pread = c["v"]
            p.drain("v")
            p.do("v", lambda h: h.tensor_tensor(
                kropef[0:64, 0:TOK], ropes[0][0:64, 0:TOK],
                ropes[1][0:64, 0:TOK], ALU.subtract))
            p.do("v", lambda h: h.tensor_tensor(
                kropef[64:128, 0:TOK], ropes[0][64:128, 0:TOK],
                ropes[1][64:128, 0:TOK], ALU.add))
            p.drain("v")
            p.do("v", lambda h: h.tensor_scalar(
                ropes[0][:, 0:TOK], kropef[:, 0:TOK], 0.0, None, ALU.min), "v")
            mn = c["v"]
            p.wait("s", "v", mn)
            p.drain("s")
            ex = p.do("s", lambda h: h.activation(
                ropes[1][:, 0:TOK], ropes[0][:, 0:TOK], ACT.Exp), "s")
            p.wait("v", "s", ex)
            p.drain("v")
            p.do("v", lambda h: h.scalar_tensor_tensor(
                dst, kropef[:, 0:TOK], 0.0, ropes[1][:, 0:TOK],
                ALU.max, ALU.add), "v")
            return pread, c["v"]

        hend_v = {}
        for h_ in range(H):
            wprojQ, ldq, wpiQ = wproj_load(WT["wq"][h_])
            for s_, v_ in ldq:
                p.wait("t", s_, v_)
            p.wait("t", "v", hend_v.get(h_ - 1, initv))
            for nh in range(2):
                for k in range(KT):
                    p.mm(pA[:, nh * 512:(nh + 1) * 512],
                         wprojQ[:, k * 128:(k + 1) * 128],
                         xh[:, k * TOK + nh * 512:k * TOK + (nh + 1) * 512],
                         start=(k == 0), stop=(k == KT - 1))
            qmm = c["t"]
            wproj_done(wpiQ)
            wprojK, ldk, wpiK = wproj_load(WT["wk"][h_])
            for s_, v_ in ldk:
                p.wait("t", s_, v_)
            for nh in range(2):
                for k in range(KT):
                    p.mm(pA[:, 1024 + nh * 512:1024 + (nh + 1) * 512],
                         wprojK[:, k * 128:(k + 1) * 128],
                         xh[:, k * TOK + nh * 512:k * TOK + (nh + 1) * 512],
                         start=(k == 0), stop=(k == KT - 1))
            kmm = c["t"]
            wproj_done(wpiK)
            wprojV, ldv, wpiV = wproj_load(WT["wv"][h_])
            for s_, v_ in ldv:
                p.wait("t", s_, v_)
            for nh in range(2):
                for k in range(KT):
                    p.mm(pB[:, nh * 512:(nh + 1) * 512],
                         wprojV[:, k * 128:(k + 1) * 128],
                         xh[:, k * TOK + nh * 512:k * TOK + (nh + 1) * 512],
                         start=(k == 0), stop=(k == KT - 1))
            vmm = c["t"]
            wproj_done(wpiV)

            p.wait("v", "t", qmm)
            _, qphi_v = rope_phi(pA[:, 0:1024], qphi[:, h_ * TOK:(h_ + 1) * TOK],
                                 h_)
            p.wait("v", "t", kmm)
            _, kphi_v = rope_phi(pA[:, 1024:2048], kfm[:, 0:TOK], h_)
            if "p1a" in stages and h_ == 0:
                wait_all_dma("y")
                return
            vfm_bf = ropes[0][:].bitcast(BF16)
            vts = kropef[:].bitcast(BF16)
            p.wait("v", "t", vmm)
            p.drain("v")
            p.do("v", lambda h: h.tensor_copy(vfm_bf[:, 0:TOK], pB[:, 0:TOK]))
            p.drain("v")
            p.do("v", lambda h: h.memset(
                vts[:, 0:NCH * 132].rearrange("p (n e) -> p n e", e=132)[:, :, 128:129],
                1.0), "v")
            vts_init = c["v"]
            p.wait("t", "v", vts_init)
            vcp = {}
            for c_ in range(NCH):
                if c_ >= 2:
                    p.wait("t", "v", vcp[c_ - 2])
                ttp = p.tp(pw_T[c_ % 2], vfm_bf[:, c_ * 128:(c_ + 1) * 128],
                           id128b[:])
                p.wait("v", "t", ttp)
                p.drain("v")
                p.do("v", lambda h, cc=c_: h.tensor_copy(
                    vts[:, cc * 132:cc * 132 + 128], pw_T[cc % 2]), "v")
                vcp[c_] = c["v"]
            vts_done = c["v"]
            if "p1b" in stages and h_ == 0:
                wait_all_dma("y")
                return

            mask_v, ks_v, st_v, kvt, ymt = {}, {}, {}, {}, {}
            nch_h = 0 if ("noc%d" % h_) in stages else NCH
            for c_ in range(nch_h + 1 if nch_h else 0):
                if c_ < nch_h:
                    pw = c_ % 2
                    ii = idx(c_, h_)
                    if c_ == 0:
                        p.wait("t", "v", max(qphi_v, kphi_v, vts_done))
                    if "nos" not in stages:
                        if c_ >= 2:
                            p.wait("t", "v", mask_v[c_ - 2])
                        smm = p.mm(pw_S[pw], kfm[:, c_ * 128:(c_ + 1) * 128],
                                   qphi[:, h_ * TOK + c_ * 128:h_ * TOK + (c_ + 1) * 128])
                        p.wait("v", "t", smm)
                        p.drain("v")
                        p.do("v", lambda h, i_=ii, w=pw: h.scalar_tensor_tensor(
                            ascrP[w][:], pw_S[w], rpcol[:, i_:i_ + 1], maskt[:],
                            ALU.mult, ALU.mult), "v")
                        mask_v[c_] = c["v"]
                    else:
                        mask_v[c_] = c["v"]
                    if "nok" not in stages:
                        if c_ >= 2:
                            p.wait("t", "v", ks_v[c_ - 2])
                        ktp = p.tp(pw_T[pw], kfm[:, c_ * 128:(c_ + 1) * 128],
                                   id128b[:])
                        p.wait("v", "t", ktp)
                        p.drain("v")
                        p.do("v", lambda h, i_=ii, w=pw: h.tensor_scalar(
                            ksbP[w][:], pw_T[w], rpcol[:, i_:i_ + 1], None,
                            ALU.mult), "v")
                        ks_v[c_] = c["v"]
                    else:
                        ks_v[c_] = c["v"]
                if c_ >= 1:
                    cc = c_ - 1
                    pw = cc % 2
                    ii = idx(cc, h_)
                    if "nokv" not in stages:
                        p.wait("t", "v", ks_v[cc])
                        if cc >= 2:
                            p.wait("t", "v", st_v[cc - 2])
                        kvmm = p.mm(pw_K[pw], ksbP[pw][:],
                                    vts[:, cc * 132:cc * 132 + 129])
                        kvt[cc] = c["t"]
                    if "noy" not in stages:
                        if cc >= 1:
                            p.wait("t", "v", st_v[cc - 1])
                        p.mm(pw_Y[pw], ascrP[pw][:],
                             vts[:, cc * 132:cc * 132 + 129],
                             start=True, stop=False)
                        p.mm(pw_Y[pw],
                             qphi[:, h_ * TOK + cc * 128:h_ * TOK + (cc + 1) * 128],
                             st_sl(state1, h_), start=False, stop=True)
                        ymt[cc] = c["t"]
                    if "nokv" not in stages:
                        p.wait("v", "t", kvt[cc])
                        p.drain("v")
                        p.do("v", lambda h, hh=h_, ci=cc: h.tensor_copy(
                            kv_sl(hh, ci), pw_K[ci % 2]))
                        p.drain("v")
                        p.do("v", lambda h, hh=h_, ci=cc: h.tensor_tensor(
                            fscr, st_sl(state1, hh), kv_sl(hh, ci), ALU.add), "v")
                        if "noy" not in stages:
                            p.wait("v", "t", ymt[cc])
                        p.drain("v")
                        p.do("v", lambda h, hh=h_, i_=ii: h.tensor_scalar(
                            st_sl(state1, hh), fscr, slast[:, i_:i_ + 1], None,
                            ALU.mult), "v")
                        p.drain("v")
                    if "noy" not in stages:
                        p.wait("v", "t", ymt[cc])
                        p.drain("v")
                        p.do("v", lambda h, hh=h_, ci=cc: h.tensor_copy(
                            yint_sl(hh, ci), pw_Y[ci % 2]), "v")
                    st_v[cc] = c["v"]
            hend_v[h_] = c["v"]
            if "p1h%d" % h_ in stages:
                wait_all_dma("y")
                return
        if "p1" in stages:
            wait_all_dma("y")
            return

        p.drain("v")
        p.do("v", lambda h: h.sem_inc(p.sems["bv"], 1))
        c["bv"] += 1
        p.wait("y", "bv", c["bv"])
        p.dma("y", CC_IN.rearrange("(hh pp) e -> pp hh e", pp=128),
              state1[:, 0:H * 132].rearrange("p (hh e) -> p hh e", e=132),
              sem="dcc")
        ccin_d = c["dcc"]
        p.wait("g", "dcc", ccin_d)
        p.do("g", lambda h: h.collective_compute(
            "AllGather", ALU.bypass, ins=[CC_IN], outs=[CC_OUT],
            replica_groups=groups), "cc", 1)
        p.wait("y", "cc", c["cc"])
        p.dma("y", staterecv[:, 0:H * 132].rearrange("p (hh e) -> p hh e", e=132),
              CC_OUT[0:H * 128, :].rearrange("(hh pp) e -> pp hh e", pp=128),
              sem="dcc")
        ccout_d = c["dcc"]
        p.wait("v", "dcc", ccout_d)
        p.drain("v")
        p.do("v", lambda h: h.tensor_scalar(
            state2[:, 0:H * 132], staterecv[:, 0:H * 132], mrecv[:], None,
            ALU.mult), "v")
        s2init = c["v"]
        p.wait("t", "v", s2init)
        if "cc1" in stages:
            wait_all_dma("y")
            return

        yfm = xh
        picol = ropes[0]
        tmpi = p.mm(pA[:, 1536:1664], onesrow[:], prow[0:1, 128:256])
        p.wait("v", "t", tmpi)
        p.drain("v")
        p.do("v", lambda h: h.tensor_copy(picol[:, 0:128], pA[:, 1536:1664]), "v")
        pic_v = c["v"]
        p.wait("t", "v", pic_v)
        uP = [qfmf[:, 256:388], qfmf[:, 388:520]]
        ytmpP = ksbP
        pw_tp = [pB_tpy, pB_tpv]
        order2 = [(hh, cc) for hh in range(H) for cc in range(NCH)]
        immt, ytp_t, yn_v = {}, {}, {}
        for j, (h_, c_) in enumerate(order2):
            ii = idx(c_, h_)
            pw = j % 2
            if j >= 2:
                p.wait("t", "v", yn_v[j - 2])
            p.mm(pA[:, pw * 512:pw * 512 + 129],
                 qphi[:, h_ * TOK + c_ * 128:h_ * TOK + (c_ + 1) * 128],
                 st_sl(state2, h_))
            immt[j] = c["t"]
            if j >= 1:
                p.wait("t", "v", yn_v[j - 1])
                ytp_t[j - 1] = p.tp(pw_tp[1 - pw], ytmpP[1 - pw][:], id128b[:])
            p.wait("v", "t", immt[j])
            p.drain("v")
            p.do("v", lambda h, w=pw, i_=ii, hh=h_, ci=c_: h.scalar_tensor_tensor(
                uP[w][:, 0:129], pA[:, w * 512:w * 512 + 129],
                picol[:, i_:i_ + 1], yint_sl(hh, ci), ALU.mult, ALU.add))
            p.drain("v")
            p.do("v", lambda h, w=pw, i_=ii: h.tensor_scalar(
                denc[:, w * 4:w * 4 + 1], uP[w][:, 128:129],
                pcol[:, i_:i_ + 1], 1e-6, ALU.mult, ALU.add))
            p.drain("v")
            p.do("v", lambda h, w=pw: h.reciprocal(
                denc[:, w * 4 + 1:w * 4 + 2], denc[:, w * 4:w * 4 + 1]))
            p.drain("v")
            p.do("v", lambda h, w=pw, i_=ii: h.tensor_tensor(
                denc[:, w * 4 + 2:w * 4 + 3], denc[:, w * 4 + 1:w * 4 + 2],
                pcol[:, i_:i_ + 1], ALU.mult))
            if j >= 2:
                p.wait("v", "t", ytp_t[j - 2])
            p.drain("v")
            p.do("v", lambda h, w=pw: h.tensor_scalar(
                ytmpP[w][:], uP[w][:, 0:128], denc[:, w * 4 + 2:w * 4 + 3],
                None, ALU.mult), "v")
            yn_v[j] = c["v"]
            if j >= 1:
                jp = j - 1
                hp, cp = order2[jp]
                p.wait("v", "t", ytp_t[jp])
                p.drain("v")
                p.do("v", lambda h, hh=hp, ci=cp, w=1 - pw: h.tensor_copy(
                    yfm[:, hh * TOK + ci * 128:hh * TOK + (ci + 1) * 128],
                    pw_tp[w]), "v")
        jl = len(order2) - 1
        hp, cp = order2[jl]
        p.wait("t", "v", c["v"])
        ytp_t[jl] = p.tp(pw_tp[jl % 2], ytmpP[jl % 2][:], id128b[:])
        p.wait("v", "t", ytp_t[jl])
        p.drain("v")
        p.do("v", lambda h, hh=hp, ci=cp: h.tensor_copy(
            yfm[:, hh * TOK + ci * 128:hh * TOK + (ci + 1) * 128],
            pw_tp[jl % 2]), "v")
        yfm_done = c["v"]
        p.wait("t", "v", yfm_done)
        p.wait("g", "v", yfm_done)
        if "p2" in stages:
            wait_all_dma("y")
            return

        wait_all_dma("y")
        for ot in range(KT):
            wproj, ldw, wpi = wproj_load(WT["wo"][ot])
            for s_, v_ in ldw:
                p.wait("t", s_, v_)
            p.wait("t", "v", c["v"])
            for nh in range(2):
                for k in range(KT):
                    p.mm(wA(nh), wproj[:, k * 128:(k + 1) * 128],
                         yfm[:, k * TOK + nh * 512:k * TOK + (nh + 1) * 512],
                         start=(k == 0), stop=(k == KT - 1))
            omm = c["t"]
            wproj_done(wpi)
            xb = xtile[ot % 2]
            dxs = "dx0" if ot % 2 == 0 else "dx1"
            if ot == 0:
                p.wait("y", "v", c["v"])
            if ot >= 2:
                p.wait("y", "v", c["v"])
            p.dma("y", xb[:], X_SAVE.rearrange("(k p) t -> k p t", p=128)[ot],
                  sem=dxs)
            ldx = c[dxs]
            p.wait("v", "t", omm)
            p.wait("v", dxs, ldx)
            p.wait("v", "dst", c["dst"])
            p.drain("v")
            p.do("v", lambda h, o=ot, b=xb: h.scalar_tensor_tensor(
                tscr[0][:, 0:TOK], pA[:, 0:1024],
                bocol[:, o:o + 1], b[:], ALU.add, ALU.add), "v")
            wo_v = c["v"]
            p.wait("y", "v", wo_v)
            p.dma("y", X_SAVE.rearrange("(k p) t -> k p t", p=128)[ot],
                  tscr[0][:, 0:TOK], sem="dst")
        p.barrier()

        if "ffn2" not in stages:
            wait_all_dma("y")
            return
        ffn(X_SAVE, OUT)

        wait_all_dma("y")

    with nc.Block() as block:
        @block.tensor
        def _(tensor):
            script(P("t", tensor, sems))

        @block.vector
        def _(vector):
            script(P("v", vector, sems))

        @block.scalar
        def _(scalar):
            script(P("s", scalar, sems))

        @block.gpsimd
        def _(gpsimd):
            script(P("g", gpsimd, sems))

        @block.sync
        def _(sync):
            script(P("y", sync, sems))

    es.close()
    return nc


def _host_inputs(inputs, n_cores=8):
    x = np.ascontiguousarray(np.asarray(inputs["L0_attn_x"], dtype=np.float32))

    def tiled(w):
        w = np.asarray(w, np.float32)
        t = w.reshape(KT, 128, KT, 128).transpose(2, 1, 0, 3)
        return np.ascontiguousarray(t)

    wq_t = tiled(inputs["L0_attn_Wq"])
    wk_t = tiled(inputs["L0_attn_Wk"])
    wv_t = tiled(inputs["L0_attn_Wv"])
    wo_t = tiled(inputs["L0_attn_Wo"])
    wg = np.ascontiguousarray(np.asarray(inputs["L0_attn_Wg"], np.float32))
    bg = np.ascontiguousarray(np.asarray(inputs["L0_attn_bg"], np.float32).reshape(H, 1))
    bo = np.ascontiguousarray(
        np.asarray(inputs["L0_attn_bo"], np.float32).reshape(KT, 128).T)
    wgt = np.asarray(inputs["L0_ffn_w_gate"], np.float32).T
    wut = np.asarray(inputs["L0_ffn_w_up"], np.float32).T
    wdt = np.asarray(inputs["L0_ffn_w_down"], np.float32).T
    qg = KT // n_cores
    qd = -(-FT // n_cores)
    d_counts = [FT // n_cores + (1 if i < FT % n_cores else 0)
                for i in range(n_cores)]
    d_start = [sum(d_counts[:i]) for i in range(n_cores)]

    maskt = (np.arange(128)[:, None] <= np.arange(128)[None, :]).astype(np.float32)
    import ml_dtypes
    id128b = np.eye(128, dtype=ml_dtypes.bfloat16)
    id128f = np.eye(128, dtype=np.float32)
    id16f = np.eye(16, dtype=np.float32)
    onescol = np.ones((128, 1), np.float32)
    ones128 = np.ones((128, 128), np.float32)
    constt = np.array([[1e-5, 1.0 / 128.0, 0.0, 0.0]], np.float32)
    onesrow = np.ones((1, 128), np.float32)

    half = 64
    inv = 1.0 / (10000.0 ** (np.arange(half, dtype=np.float64) / half))

    maps = []
    for core in range(n_cores):
        b, hf = core // 2, core % 2
        t0 = hf * TOK
        pos = np.arange(t0, t0 + TOK, dtype=np.float64)
        fr = np.outer(inv, pos)
        cosv = np.cos(fr).astype(np.float32)
        sinv = np.sin(fr).astype(np.float32)
        trigc = np.ascontiguousarray(np.concatenate([cosv, cosv], 0))
        trigs = np.ascontiguousarray(np.concatenate([sinv, sinv], 0))
        x_fm = np.ascontiguousarray(x[b, t0:t0 + TOK, :].T)
        mr = np.full((128, 1), float(hf), np.float32)
        wown_g = np.ascontiguousarray(
            wgt[core * qg * 128:(core + 1) * qg * 128])
        wown_u = np.ascontiguousarray(
            wut[core * qg * 128:(core + 1) * qg * 128])
        wown_d = np.zeros((qd * 128, D), np.float32)
        nd = d_counts[core]
        wown_d[:nd * 128] = wdt[d_start[core] * 128:(d_start[core] + nd) * 128]
        maps.append(dict(
            x_fm=x_fm, wq=wq_t, wk=wk_t, wv=wv_t, wo=wo_t, wg=wg, bg=bg, bo=bo,
            wown_g=wown_g, wown_u=wown_u, wown_d=wown_d,
            trigc=trigc, trigs=trigs, maskt=maskt,
            id128b=id128b, id128f=id128f, id16f=id16f, onescol=onescol,
            ones128=ones128, onesrow=onesrow, constt=constt,
            mrecv=mr))
    return maps


def kernel(**inputs):
    n_cores = 8
    maps = _host_inputs(inputs, n_cores)
    nc = build_program(n_cores)
    res = run_bass_kernel_spmd(nc, maps, core_ids=list(range(n_cores)))
    x = np.asarray(inputs["L0_attn_x"])
    B, L, _ = x.shape
    out = np.empty((B, L, D), np.float32)
    for core in range(n_cores):
        b, hf = core // 2, core % 2
        out[b, hf * TOK:(hf + 1) * TOK, :] = np.asarray(
            res.results[core]["out_fm"]).T
    return out


# version 22 (final)
# speedup vs baseline: 1.0192x; 1.0065x over previous
import contextlib

import numpy as np
import concourse.bass as bass
import concourse.mybir as mybir
from concourse.bass_utils import run_bass_kernel_spmd

F32 = mybir.dt.float32
BF16 = mybir.dt.bfloat16
ALU = mybir.AluOpType
ACT = mybir.ActivationFunctionType

MAGIC = float(np.float32(1.5 * 2.0**23))
D = 2048
KT = 16
DFF = 5504
FT = 43
H = 16
TOK = 1024
NCH = 8
FC = 512
NFC = 2


class P:
    def __init__(self, cur, handle, sems):
        self.cur = cur
        self.h = handle
        self.sems = sems
        self.cnt = {name: 0 for name in sems}

    def on(self, eng):
        return eng == self.cur

    def do(self, eng, fn, sem=None, inc=1):
        if self.on(eng):
            instr = fn(self.h)
            if sem is not None:
                instr.then_inc(self.sems[sem], inc)
        if sem is not None:
            self.cnt[sem] += inc
            return self.cnt[sem]
        return None

    def dma(self, eng, out, in_, sem="d"):
        return self.do(eng, lambda h: h.dma_start(out=out, in_=in_), sem, 16)

    def wait(self, eng, sem, val):
        assert val <= self.cnt[sem], f"future wait {sem}:{val}>{self.cnt[sem]}"
        if self.on(eng):
            self.h.wait_ge(self.sems[sem], val)

    def drain(self, eng):
        if self.on(eng):
            self.h.drain()

    def mm(self, out, lhsT, rhs, start=True, stop=True, sem="t"):
        return self.do("t", lambda h: h.matmul(out, lhsT, rhs, start=start, stop=stop), sem)

    def tp(self, out, in_, ident, sem="t"):
        return self.do("t", lambda h: h.transpose(out, in_, ident), sem)

    def barrier(self, engs="tvsg"):
        for e in engs:
            self.drain(e)
            self.do(e, lambda h: h.sem_inc(self.sems["b" + e], 1))
            self.cnt["b" + e] += 1
        for e in engs:
            for e2 in engs:
                if e2 != e:
                    self.wait(e, "b" + e2, self.cnt["b" + e2])


def build_program(n_cores, stages=("ffn1", "attn", "ffn2"), debug=False):
    nc = bass.Bass("TRN2", target_bir_lowering=False)

    X = nc.dram_tensor("x_fm", [D, TOK], F32, kind="ExternalInput").ap()
    OUT = nc.dram_tensor("out_fm", [D, TOK], F32, kind="ExternalOutput").ap()
    WT = {nm: nc.dram_tensor(nm, [KT, 128, KT, 128], F32, kind="ExternalInput").ap()
          for nm in ("wq", "wk", "wv", "wo")}
    WG = nc.dram_tensor("wg", [D, H], F32, kind="ExternalInput").ap()
    BG = nc.dram_tensor("bg", [H, 1], F32, kind="ExternalInput").ap()
    BO = nc.dram_tensor("bo", [128, KT], F32, kind="ExternalInput").ap()
    QG = KT // n_cores
    QD = -(-FT // n_cores)
    d_counts = [FT // n_cores + (1 if i < FT % n_cores else 0)
                for i in range(n_cores)]
    d_slot = []
    for ci in range(n_cores):
        for l in range(d_counts[ci]):
            d_slot.append(ci * QD + l)
    WFFN = {"g": nc.dram_tensor("wown_g", [QG * 128, DFF], F32,
                                kind="ExternalInput").ap(),
            "u": nc.dram_tensor("wown_u", [QG * 128, DFF], F32,
                                kind="ExternalInput").ap(),
            "d": nc.dram_tensor("wown_d", [QD * 128, D], F32,
                                kind="ExternalInput").ap()}
    TRIGC = nc.dram_tensor("trigc", [128, TOK], F32, kind="ExternalInput").ap()
    TRIGS = nc.dram_tensor("trigs", [128, TOK], F32, kind="ExternalInput").ap()
    MASKT = nc.dram_tensor("maskt", [128, 128], F32, kind="ExternalInput").ap()
    ID128B = nc.dram_tensor("id128b", [128, 128], BF16, kind="ExternalInput").ap()
    ID128F = nc.dram_tensor("id128f", [128, 128], F32, kind="ExternalInput").ap()
    ID16F = nc.dram_tensor("id16f", [16, 16], F32, kind="ExternalInput").ap()
    ONESCOL = nc.dram_tensor("onescol", [128, 1], F32, kind="ExternalInput").ap()
    ONES128 = nc.dram_tensor("ones128", [128, 128], F32, kind="ExternalInput").ap()
    ONESROW = nc.dram_tensor("onesrow", [1, 128], F32, kind="ExternalInput").ap()
    MRECV = nc.dram_tensor("mrecv", [128, 1], F32, kind="ExternalInput").ap()
    CONSTT = nc.dram_tensor("constt", [1, 4], F32, kind="ExternalInput").ap()
    CCQ_IN = {t: nc.dram_tensor("ccq_in_" + t, [q * nt * 128, 128], BF16,
                                kind="Internal").ap()
              for t, q, nt in (("g", QG, FT), ("u", QG, FT), ("d", QD, KT))}
    CCQ_OUT = {t: nc.dram_tensor("ccq_out_" + t,
                                 [n_cores * q * nt * 128, 128], BF16,
                                 kind="Internal",
                                 addr_space="Shared" if n_cores > 4 else "Local").ap()
               for t, q, nt in (("g", QG, FT), ("u", QG, FT), ("d", QD, KT))}
    X_SAVE = nc.dram_tensor("x_save", [D, TOK], F32, kind="Internal").ap()
    ROWRS = nc.dram_tensor("rowrs", [2, DFF], F32, kind="Internal").ap()
    RRS2 = nc.dram_tensor("rrs2", [2, DFF], F32, kind="Internal").ap()
    CC_IN = nc.dram_tensor("cc_in", [H * 128, 132], BF16, kind="Internal").ap()
    CC_OUT = nc.dram_tensor("cc_out", [2 * H * 128, 132], BF16, kind="Internal").ap()
    DBG = {}
    if debug:
        for nm in ("qphi0", "kfm0", "vts0", "ascr00", "yint00", "kv00",
                   "st10", "st1f", "yscr00", "denc00", "yfm0"):
            DBG[nm] = nc.dram_tensor("dbg_" + nm, [128, 2112], F32, kind="Internal").ap()
    groups = [[2 * i, 2 * i + 1] for i in range(max(1, n_cores // 2))]

    es = contextlib.ExitStack()

    def sb(name, p_, f, dt=F32):
        return es.enter_context(nc.sbuf_tensor(name, [p_, f], dt))

    trigc = sb("trigc_t", 128, TOK)
    trigs = sb("trigs_t", 128, TOK)
    maskt = sb("maskt_t", 128, 128)
    id128b = sb("id128b_t", 128, 128, BF16)
    id128f = sb("id128f_t", 128, 128)
    id16f = sb("id16f_t", 16, 16)
    onescol = sb("onescol_t", 128, 1)
    ones128 = sb("ones128_t", 128, 128)
    onesrow = sb("onesrow_t", 1, 128)
    mrecv = sb("mrecv_t", 128, 1)
    bgcol = sb("bgcol", H, 1)
    bocol = sb("bocol", 128, KT)
    wgtil = sb("wgtil", 128, KT * H, BF16)
    rows = sb("rows", 1, 2048)
    colsc = sb("colsc", 128, 64)
    colsc2 = sb("colsc2", 128, 64)
    denc = sb("denc", 128, 8)
    constt = sb("constt_t", 1, 4)
    nc.const_aps.register(F32, 1e-5, constt[0:1, 0:1])
    nc.const_aps.register(F32, 1.0 / 128.0, constt[0:1, 1:2])
    prow = sb("prow", 1, 256)
    yscr = sb("yscr", 128, 132, BF16)

    AR1 = sb("AR1", 128, 22016)
    AR2 = sb("AR2", 128, 16640)
    AR3 = sb("AR3", 128, 8704)
    AR1b = AR1[:].bitcast(BF16)
    AR2b = AR2[:].bitcast(BF16)
    AR3b = AR3[:].bitcast(BF16)

    def f1(a, b_):
        return AR1[:, a:b_]

    wslab = [f1(0, 5504), f1(5504, 11008)]
    awslab = f1(11008, 16512)
    uslab = f1(16512, 22016)
    wqslab = [AR2b[:, 0:5504], AR2b[:, 5504:11008]]
    srow_b = [AR3[0:1, 0:5504], AR3[32:33, 0:5504]]

    t1buf = f1(0, 16384)
    xres = t1buf
    hrbuf_b = AR1b
    g1buf = AR2b[:, 0:16384]
    wbG = [AR2b[:, 16384:18432], AR2b[:, 18432:20480]]
    wbU = [AR2b[:, 20480:22528], AR2b[:, 22528:24576]]
    wbD = [AR2b[:, 16384:21888], AR2b[:, 21888:27392]]
    xtile = [AR2[:, 13696:14720], AR2[:, 14720:15744]]
    tscr = [AR3[:, 0:1024], AR3[:, 1024:2048]]
    amaxt = AR3[:, 2048:3072]
    rowbM = AR3[:, 3072:4096]
    rowbR = AR3[:, 4096:5120]
    rowbS = AR3[:, 5120:6144]
    rowbI = AR3[:, 6144:7168]
    rowbF = AR3[:, 7168:8192]
    recipBq = AR2[:, 5504:11008]
    scaleBq = AR2[:, 11008:16512]

    ropes = [f1(0, 1024), f1(1024, 2048)]
    qfmf = f1(2048, 3072)
    kropef = f1(3072, 4096)
    pcol = f1(4096, 4224)
    rpcol = f1(4224, 4352)
    rpscr = f1(4352, 4480)
    slast = f1(4480, 4608)
    kfm = AR1b[:, 9216:11264]
    wproj_bufs = [AR1b[:, 11264:13312], AR1b[:, 40896:42944]]
    state1 = AR1b[:, 13312:15424]
    staterecv = AR1b[:, 15424:17536]
    state2 = AR1b[:, 17536:19648]
    beta = AR1[0:16, 9824:10848]
    pbuf = AR1[0:16, 10848:11872]
    z16 = AR1[0:16, 11872:12000]
    yint = AR1b[:, 24000:40896]
    xh = AR2b[:, 0:16384]
    kvbuf = AR2b[:, 16384:33280]
    qphi = AR3b[:, 0:16384]

    pA = es.enter_context(nc.psum_tensor("pA", [128, 2048], F32))
    pB = es.enter_context(nc.psum_tensor("pB", [128, 2048], F32))

    def wA(i, w=512, p0=0, p1=128):
        return pA[p0:p1, i * 512:i * 512 + w]

    def wB(i, w=512, p0=0, p1=128):
        return pB[p0:p1, i * 512:i * 512 + w]

    pB_tpv = pB[:, 1024:1088].bitcast(BF16)
    pB_tpy = pB[:, 512:576].bitcast(BF16)

    sems = {n: es.enter_context(nc.semaphore("sem_" + n))
            for n in (["d", "g2d", "cc", "t", "v", "s", "g",
                       "dx0", "dx1"] + ["dw%d" % i for i in range(8)] +
                      ["dsa", "dsb", "dqr", "dbc", "dcc", "dst"] +
                      ["g2d%s" % ch for ch in "abcdefgh"] +
                      ["b" + e for e in "tvsg"])}
    YDMA_SEMS = (["d", "dx0", "dx1"] + ["dw%d" % i for i in range(8)] +
                 ["dsa", "dsb", "dbc", "dcc", "dst"])

    def st_sl(buf, h_):
        return buf[:, h_ * 132:h_ * 132 + 129]

    def kv_sl(h_, c_):
        return kvbuf[:, (h_ * NCH + c_) * 132:(h_ * NCH + c_) * 132 + 129]

    def yint_sl(h_, c_):
        return yint[:, (h_ * NCH + c_) * 132:(h_ * NCH + c_) * 132 + 129]

    def idx(c_, h_):
        return c_ * 16 + h_

    def script(p: P):
        c = p.cnt

        def dump(nm, ap, eng="v"):
            if not debug:
                return
            p.drain(eng)
            p.do(eng, lambda h: h.sem_inc(p.sems[eng], 1))
            c[eng] += 1
            p.wait("y", eng, c[eng])
            n = ap.shape[-1]
            np_ = ap.shape[0]
            p.dma("y", DBG[nm][0:np_, 0:n], ap, sem="d")
            p.wait("y", "d", c["d"])
            p.wait(eng, "d", c["d"])

        def wait_all_dma(eng):
            for s_ in YDMA_SEMS:
                p.wait(eng, s_, c[s_])
            p.wait(eng, "g2d", c["g2d"])
            for ch_ in "abcdefgh":
                p.wait(eng, "g2d" + ch_, c["g2d" + ch_])
            p.wait(eng, "dqr", c["dqr"])

        def recip_acc(out_ap, in_ap, scr_ap, sem=None):
            return p.do("v", lambda h: h.reciprocal(out_ap, in_ap), sem)

        for dst, src in ((trigc, TRIGC), (trigs, TRIGS), (maskt, MASKT),
                         (id128b, ID128B), (id128f, ID128F), (id16f, ID16F),
                         (onescol, ONESCOL), (ones128, ONES128),
                         (constt, CONSTT),
                         (onesrow, ONESROW), (mrecv, MRECV), (bgcol, BG),
                         (bocol, BO)):
            p.dma("y", dst[:], src)
        p.do("g", lambda h: h.dma_start(
            out=wgtil[:].rearrange("p (k hh) -> p k hh", k=KT),
            in_=WG.rearrange("(k p) hh -> p k hh", p=128)), "g2d", 16)
        ld0, g2d0 = c["d"], c["g2d"]
        for e in "tvsg":
            p.wait(e, "d", ld0)
            p.wait(e, "g2d", g2d0)
        p.wait("y", "g2d", g2d0)

        def quant(tag, n_groups, n_out):
            WSRC = WFFN[tag]
            nspan = (n_out + 2047) // 2048
            ncc = n_out // 128
            CIN4 = CCQ_IN[tag].rearrange("(l ot p) m -> l ot p m",
                                         ot=ncc, p=128)
            vdone, wqd, chain_d, scd, lds, absd = {}, {}, {}, {}, {}, {}
            span_rd, span_bc, srow_s = {}, {}, {}
            win_ev = {0: 0, 1: 0, 2: 0, 3: 0}
            q4 = [n_out // 4 * i for i in range(4)] + [n_out]

            def sA_load(g):
                buf = g % 2
                dss = "dsa" if buf == 0 else "dsb"
                if g >= 2:
                    p.wait("y", "v", vdone[g - 2])
                    p.wait("y", dss, wqd.get(g - 2, 0))
                src_ = WSRC.rearrange("(g p) o -> g p o", p=128)[g]
                for i in range(4):
                    p.dma("y", wslab[buf][:, q4[i]:q4[i + 1]],
                          src_[:, q4[i]:q4[i + 1]], sem="dw%d" % (4 * buf + i))
                lds[g] = [("dw%d" % (4 * buf + i), c["dw%d" % (4 * buf + i)])
                          for i in range(4)]

            def sB_bcast(g):
                buf = g % 2
                p.wait("s", "dqr", chain_d[g])
                for sp in range(nspan):
                    o0 = sp * 2048
                    wsp = min(2048, n_out - o0)
                    p.wait("s", "v", span_rd.get((g - 1, sp), 0))
                    p.wait("s", "dbc", c["dbc"])
                    p.do("s", lambda h, b=buf, o_=o0, w_=wsp: h.dma_start(
                        out=recipBq[:, o_:o_ + w_],
                        in_=ROWRS[b:b + 1, o_:o_ + w_].partition_broadcast(128)[:, 0, :]),
                        "dbc", 16)
                    p.do("s", lambda h, b=buf, o_=o0, w_=wsp: h.dma_start(
                        out=scaleBq[:, o_:o_ + w_],
                        in_=RRS2[b:b + 1, o_:o_ + w_].partition_broadcast(128)[:, 0, :]),
                        "dbc", 16)
                    span_bc[(g, sp)] = c["dbc"]

            def sB_spans(g):
                buf = g % 2
                dss = "dsa" if buf == 0 else "dsb"
                for sp in range(nspan):
                    o0 = sp * 2048
                    wsp = min(2048, n_out - o0)
                    if sp == 0 and g >= 2:
                        p.wait("v", dss, wqd[g - 2])
                    p.wait("v", "dbc", span_bc[(g, sp)])
                    if sp == 0:
                        for s_, v_ in lds[g]:
                            p.wait("v", s_, v_)
                    p.drain("v")
                    p.do("v", lambda h, b=buf, o_=o0, w_=wsp: h.tensor_tensor(
                        uslab[:, o_:o_ + w_],
                        wslab[b][:, o_:o_ + w_], recipBq[:, o_:o_ + w_],
                        ALU.mult))
                    p.drain("v")
                    p.do("v", lambda h, o_=o0, w_=wsp: h.tensor_scalar(
                        uslab[:, o_:o_ + w_], uslab[:, o_:o_ + w_],
                        MAGIC, MAGIC, ALU.add, ALU.subtract))
                    p.drain("v")
                    p.do("v", lambda h, o_=o0, w_=wsp: h.tensor_scalar(
                        uslab[:, o_:o_ + w_], uslab[:, o_:o_ + w_],
                        -1.0, 1.0, ALU.max, ALU.min))
                    p.drain("v")
                    p.do("v", lambda h, b=buf, o_=o0, w_=wsp: h.tensor_tensor(
                        wqslab[b][:, o_:o_ + w_], uslab[:, o_:o_ + w_],
                        scaleBq[:, o_:o_ + w_], ALU.mult), "v")
                    span_rd[(g, sp)] = c["v"]
                vdone[g] = c["v"]

            def sA_abs(g):
                buf = g % 2
                for s_, v_ in lds[g]:
                    p.wait("s", s_, v_)
                p.wait("s", "t", scd.get(g - 1, 0))
                p.drain("s")
                p.do("s", lambda h, b=buf, no=n_out: h.activation(
                    awslab[:, 0:no], wslab[b][:, 0:no], ACT.Abs), "s")
                absd[g] = c["s"]

            def sA_scales(g):
                buf = g % 2
                srow = srow_b[buf]
                p.wait("t", "s", absd[g])
                for oc in range((n_out + 511) // 512):
                    cw = min(512, n_out - oc * 512)
                    p.wait("t", "s", win_ev[oc % 4])
                    p.mm(wA(oc % 4, cw, p1=1), onescol[:],
                         awslab[:, oc * 512:oc * 512 + cw])
                    tmm = c["t"]
                    p.wait("s", "t", tmm)
                    p.drain("s")
                    p.do("s", lambda h, o_=oc, w0=cw, sr=srow: h.activation(
                        sr[0:1, o_ * 512:o_ * 512 + w0], wA(o_ % 4, w0, p1=1),
                        ACT.Copy, bias=1e-5, scale=1.0 / 128.0), "s")
                    win_ev[oc % 4] = c["s"]
                scd[g] = c["t"]
                srow_s[g] = c["s"]

            def sA_chain(g):
                buf = g % 2
                srow = srow_b[buf]
                p.wait("g", "s", srow_s[g])
                p.wait("g", "dqr", c["dqr"])
                p.do("g", lambda h, no=n_out, b=buf, sr=srow: h.dma_start(
                    out=ROWRS[b:b + 1, 0:no], in_=sr[0:1, 0:no]), "dqr", 16)
                p.do("g", lambda h, no=n_out, b=buf, sr=srow: h.dma_start(
                    out=RRS2[b:b + 1, 0:no], in_=sr[0:1, 0:no]), "dqr", 16)
                p.wait("g", "dqr", c["dqr"])
                cbuf = colsc if buf == 0 else colsc2
                p.do("g", lambda h, no=n_out, n=ncc, b=buf, cb=cbuf: h.dma_start(
                    out=cb[:, 0:n],
                    in_=ROWRS[b:b + 1, 0:no].rearrange(
                        "one (pp f) -> (one pp) f", pp=128)), "dqr", 16)
                d2 = c["dqr"]
                p.wait("v", "dqr", d2)
                p.drain("v")
                p.do("v", lambda h, n=ncc, cb=cbuf: h.reciprocal(
                    cb[:, 0:n], cb[:, 0:n]), "v")
                rec_v = c["v"]
                p.wait("g", "v", rec_v)
                p.do("g", lambda h, no=n_out, n=ncc, b=buf, cb=cbuf: h.dma_start(
                    out=ROWRS[b:b + 1, 0:no].rearrange(
                        "one (pp f) -> (one pp) f", pp=128),
                    in_=cb[:, 0:n]), "dqr", 16)
                chain_d[g] = c["dqr"]

            def sB_store(g):
                buf = g % 2
                dss = "dsa" if buf == 0 else "dsb"
                p.wait("y", "v", vdone[g])
                for ot in range(ncc):
                    p.dma("y", CIN4[g, ot],
                          wqslab[buf][:, ot * 128:(ot + 1) * 128], sem=dss)
                wqd[g] = c[dss]

            for it in range(n_groups + 1):
                if it < n_groups:
                    sA_load(it)
                if it >= 1:
                    sB_bcast(it - 1)
                    sB_spans(it - 1)
                if it < n_groups:
                    sA_abs(it)
                    sA_scales(it)
                    sA_chain(it)
                if it >= 1:
                    sB_store(it - 1)
            p.wait("y", "dsa", c["dsa"])
            p.wait("y", "dsb", c["dsb"])
            p.wait("y", "v", vdone[n_groups - 1])

        quant("g", QG, DFF)
        quant("u", QG, DFF)
        quant("d", QD, D)
        allg = [[i for i in range(n_cores)]]
        ccq_done = {}
        for t in ("g", "u", "d"):
            p.wait("g", "dsa", c["dsa"])
            p.wait("g", "dsb", c["dsb"])
            p.do("g", lambda h, t_=t: h.collective_compute(
                "AllGather", ALU.bypass, ins=[CCQ_IN[t_]], outs=[CCQ_OUT[t_]],
                replica_groups=allg), "cc", 1)
            ccq_done[t] = c["cc"]
        p.barrier()

        def ln_stats(XSRC):
            wait_all_dma("y")
            for k in range(KT):
                xb = xres[:, k * TOK:(k + 1) * TOK]
                dxs = "dx0" if k % 2 == 0 else "dx1"
                if k == 0:
                    p.wait("y", "t", c["t"])
                    p.wait("y", "v", c["v"])
                    p.wait("y", "s", c["s"])
                if k >= 2:
                    p.wait("y", "t", c["t"])
                p.dma("y", xb, XSRC.rearrange("(k p) t -> k p t", p=128)[k],
                      sem=dxs)
                ldx = c[dxs]
                sq = tscr[k % 2]
                p.wait("s", dxs, ldx)
                if k >= 2:
                    p.wait("s", "t", c["t"])
                p.drain("s")
                sk = p.do("s", lambda h, b=xb, q=sq: h.activation(
                    q[:, 0:TOK], b, ACT.Square), "s")
                p.wait("t", dxs, ldx)
                p.wait("t", "s", sk)
                if k == 0:
                    p.wait("t", "v", c["v"])
                first, last = (k == 0), (k == KT - 1)
                for nh in range(2):
                    p.mm(wA(0 + nh, p1=1), onescol[:],
                         xb[:, nh * 512:(nh + 1) * 512], start=first, stop=last)
                    p.mm(wA(2 + nh, p1=1), onescol[:],
                         sq[:, nh * 512:(nh + 1) * 512], start=first, stop=last)
            tdone = c["t"]
            p.wait("v", "t", tdone)
            p.wait("v", "dst", c["dst"])
            p.drain("v")
            for nh in range(2):
                p.do("v", lambda h, n=nh: h.tensor_scalar(
                    rows[0:1, n * 512:(n + 1) * 512], wA(n, p1=1),
                    1.0 / D, None, ALU.mult))
                p.do("v", lambda h, n=nh: h.tensor_scalar(
                    tscr[0][0:1, n * 512:(n + 1) * 512], wA(2 + n, p1=1),
                    1.0 / D, None, ALU.mult))
            p.drain("v")
            p.do("v", lambda h: h.tensor_tensor(
                tscr[1][0:1, 0:TOK], rows[0:1, 0:TOK], rows[0:1, 0:TOK], ALU.mult))
            p.drain("v")
            p.do("v", lambda h: h.tensor_tensor(
                tscr[0][0:1, 0:TOK], tscr[0][0:1, 0:TOK], tscr[1][0:1, 0:TOK],
                ALU.subtract), "v")
            p.drain("v")
            p.do("v", lambda h: h.tensor_scalar(
                tscr[0][0:1, 0:TOK], tscr[0][0:1, 0:TOK], 1e-5, None, ALU.add), "v")
            vv = c["v"]
            p.wait("s", "v", vv)
            p.drain("s")
            p.do("s", lambda h: h.activation(
                tscr[0][0:1, 0:TOK], tscr[0][0:1, 0:TOK], ACT.Ln), "s")
            sln = c["s"]
            p.wait("v", "s", sln)
            p.drain("v")
            p.do("v", lambda h: h.tensor_scalar(
                tscr[0][0:1, 0:TOK], tscr[0][0:1, 0:TOK], -0.5, None, ALU.mult), "v")
            vln = c["v"]
            p.wait("s", "v", vln)
            p.drain("s")
            p.do("s", lambda h: h.activation(
                rows[0:1, 1024:2048], tscr[0][0:1, 0:TOK], ACT.Exp), "s")
            sR = c["s"]
            p.wait("v", "s", sR)
            p.wait("t", "s", sR)

        def bcast(row_ap, win, dst):
            n = row_ap.shape[-1]
            p.wait("y", "v", c["v"])
            p.wait("y", "s", c["s"])
            p.wait("y", "dbc", c["dbc"])
            p.dma("y", RRS2[0:1, 0:n], row_ap, sem="dbc")
            p.wait("y", "dbc", c["dbc"])
            bc = p.dma("y", dst, RRS2[0:1, 0:n].partition_broadcast(128)[:, 0, :],
                       sem="dbc")
            return bc

        def amax_bc(dst_rowb, mul_rowb, mul_const):
            p.wait("t", "v", c["v"])
            for b_ in range(8):
                p.tp(pA[:, b_ * 128:(b_ + 1) * 128],
                     amaxt[:, b_ * 128:(b_ + 1) * 128], id128f[:])
            tpd = c["t"]
            p.wait("v", "t", tpd)
            p.drain("v")
            p.do("v", lambda h: h.tensor_copy(tscr[1][:, 0:TOK], pA[:, 0:TOK]))
            p.drain("v")
            p.do("v", lambda h: h.tensor_reduce(
                colsc[:, 0:8],
                tscr[1][:, 0:TOK].rearrange("p (b f) -> p b f", f=128),
                axis=mybir.AxisListType.X, op=ALU.max,
                apply_absolute_value=True))
            p.drain("v")
            p.do("v", lambda h: h.tensor_scalar(
                colsc[:, 0:8], colsc[:, 0:8], 1e-5, None, ALU.max))
            p.drain("v")
            recip_acc(colsc[:, 8:16], colsc[:, 0:8], colsc[:, 24:32])
            p.drain("v")
            p.do("v", lambda h: h.tensor_scalar(
                colsc[:, 16:24], colsc[:, 8:16], 127.0, None, ALU.mult), "v")
            for b_ in range(8):
                p.drain("v")
                p.do("v", lambda h, bb=b_: h.tensor_scalar(
                    tscr[1][:, 0:128], id128f[:], colsc[:, 16 + bb:17 + bb], None,
                    ALU.mult))
                p.drain("v")
                p.do("v", lambda h, bb=b_: h.tensor_scalar(
                    tscr[1][:, 128:256], id128f[:], colsc[:, 0 + bb:1 + bb], None,
                    ALU.mult), "v")
                dgv = c["v"]
                p.wait("t", "v", dgv)
                p.mm(pA[:, 1024 + b_ * 128:1152 + b_ * 128], ones128[:],
                     tscr[1][:, 0:128])
                tmm = p.mm(pB[:, b_ * 128:(b_ + 1) * 128], ones128[:],
                           tscr[1][:, 128:256])
                p.wait("v", "t", tmm)
            p.drain("v")
            p.do("v", lambda h: h.tensor_copy(rowbS[:, 0:TOK], pA[:, 1024:2048]))
            p.drain("v")
            p.do("v", lambda h: h.scalar_tensor_tensor(
                dst_rowb[:, 0:TOK], pB[:, 0:TOK], mul_const,
                mul_rowb[:, 0:TOK], ALU.mult, ALU.mult), "v")

        def ffn(XSRC, XDST):
            XS3 = XSRC.rearrange("(k p) t -> k p t", p=128)
            XD3 = XDST.rearrange("(k p) t -> k p t", p=128)
            gg4 = CCQ_OUT["g"].rearrange("(s ot p) m -> s ot p m", ot=FT, p=128)
            uu4 = CCQ_OUT["u"].rearrange("(s ot p) m -> s ot p m", ot=FT, p=128)
            dd4 = CCQ_OUT["d"].rearrange("(s ot p) m -> s ot p m", ot=KT, p=128)
            ln_stats(XSRC)
            bcast(rows[0:1, 0:TOK], None, rowbM[:, 0:TOK])
            bcR = bcast(rows[0:1, 1024:2048], None, rowbR[:, 0:TOK])
            p.wait("v", "dbc", bcR)
            t1v, absS, fold = {}, {}, {}

            def emit_fold(k):
                p.wait("v", "s", absS[k])
                p.drain("v")
                q = tscr[k % 2]
                if k == 0:
                    p.do("v", lambda h, q_=q: h.tensor_copy(
                        amaxt[:, 0:TOK], q_[:, 0:TOK]), "v")
                else:
                    p.do("v", lambda h, q_=q: h.tensor_tensor(
                        amaxt[:, 0:TOK], amaxt[:, 0:TOK], q_[:, 0:TOK],
                        ALU.max), "v")
                fold[k] = c["v"]

            for k in range(KT):
                p.drain("v")
                p.do("v", lambda h, kk=k: h.scalar_tensor_tensor(
                    t1buf[:, kk * TOK:(kk + 1) * TOK],
                    xres[:, kk * TOK:(kk + 1) * TOK], 0.0,
                    rowbM[:, 0:TOK], ALU.add, ALU.subtract), "v")
                t1v[k] = c["v"]
                p.wait("s", "v", t1v[k])
                if k >= 2:
                    p.wait("s", "v", fold[k - 2])
                p.drain("s")
                absS[k] = p.do("s", lambda h, q=tscr[k % 2], kk=k: h.activation(
                    q[:, 0:TOK], t1buf[:, kk * TOK:(kk + 1) * TOK], ACT.Abs),
                    "s")
                if k >= 1:
                    emit_fold(k - 1)
            emit_fold(KT - 1)
            amax_bc(rowbI, rowbR, 1.0 / 127.0)
            g1v = {}
            for k in range(KT):
                p.drain("v")
                p.do("v", lambda h, kk=k: h.tensor_tensor(
                    tscr[0][:, 0:TOK], t1buf[:, kk * TOK:(kk + 1) * TOK],
                    rowbS[:, 0:TOK], ALU.mult))
                p.drain("v")
                p.do("v", lambda h, kk=k: h.tensor_scalar(
                    g1buf[:, kk * TOK:(kk + 1) * TOK], tscr[0][:, 0:TOK],
                    MAGIC, MAGIC, ALU.add, ALU.subtract), "v")
                g1v[k] = c["v"]

            gld, gumm, epd, hrv, absS2, fold2 = {}, {}, {}, {}, {}, {}

            def gu_load(og):
                pg = og % 2
                if og == 0:
                    p.wait("y", "cc", ccq_done["u"])
                if og >= 2:
                    p.wait("y", "t", gumm[og - 2])
                for i, (buf, W4) in enumerate(((wbG[pg], gg4), (wbU[pg], uu4))):
                    for hf in range(2):
                        s_ = "dw%d" % (pg * 4 + i * 2 + hf)
                        for g in range(hf * 8, (hf + 1) * 8):
                            p.dma("y", buf[:, g * 128:(g + 1) * 128],
                                  W4[g, og], sem=s_)
                gld[og] = [("dw%d" % (pg * 4 + j), c["dw%d" % (pg * 4 + j)])
                           for j in range(4)]

            def emit_fold2(og):
                p.wait("s", "v", c["v"])
                if og >= 1:
                    p.wait("s", "v", fold2.get(og - 1, 0))
                p.drain("s")
                absS2[og] = p.do("s", lambda h, o=og: h.activation(
                    tscr[1][:, 0:TOK],
                    hrbuf_b[:, o * TOK:(o + 1) * TOK], ACT.Abs), "s")
                p.wait("v", "s", absS2[og])
                p.drain("v")
                if og == 0:
                    p.do("v", lambda h: h.tensor_copy(
                        amaxt[:, 0:TOK], tscr[1][:, 0:TOK]), "v")
                else:
                    p.do("v", lambda h: h.tensor_tensor(
                        amaxt[:, 0:TOK], amaxt[:, 0:TOK], tscr[1][:, 0:TOK],
                        ALU.max), "v")
                fold2[og] = c["v"]

            gu_load(0)
            for og in range(FT):
                if og + 1 < FT:
                    gu_load(og + 1)
                pg = og % 2
                for s_, v_ in gld[og]:
                    p.wait("t", s_, v_)
                if og >= 2:
                    p.wait("t", "v", epd[og - 2])
                for g in range(KT):
                    if og == 0:
                        p.wait("t", "v", g1v[g])
                    for ch in range(2):
                        p.mm(pA[:, pg * 1024 + ch * 512:pg * 1024 + ch * 512 + 512],
                             wbG[pg][:, g * 128:(g + 1) * 128],
                             g1buf[:, g * TOK + ch * 512:g * TOK + ch * 512 + 512],
                             start=(g == 0), stop=(g == KT - 1))
                    for ch in range(2):
                        p.mm(pB[:, pg * 1024 + ch * 512:pg * 1024 + ch * 512 + 512],
                             wbU[pg][:, g * 128:(g + 1) * 128],
                             g1buf[:, g * TOK + ch * 512:g * TOK + ch * 512 + 512],
                             start=(g == 0), stop=(g == KT - 1))
                gumm[og] = c["t"]
                p.wait("v", "t", gumm[og])
                p.wait("v", "s", c["s"])
                p.drain("v")
                p.do("v", lambda h, g_=pg: h.scalar_tensor_tensor(
                    tscr[0][:, 0:TOK], pA[:, g_ * 1024:(g_ + 1) * 1024], 0.0,
                    rowbI[:, 0:TOK], ALU.add, ALU.mult), "v")
                gtv = c["v"]
                p.wait("s", "v", gtv)
                if og >= 1:
                    p.wait("s", "v", fold2.get(og - 1, 0))
                p.drain("s")
                sil = p.do("s", lambda h: h.activation(
                    tscr[1][:, 0:TOK], tscr[0][:, 0:TOK], ACT.Silu), "s")
                p.wait("v", "s", sil)
                p.drain("v")
                p.do("v", lambda h, g_=pg, o=og: h.scalar_tensor_tensor(
                    hrbuf_b[:, o * TOK:(o + 1) * TOK],
                    pB[:, g_ * 1024:(g_ + 1) * 1024], 0.0,
                    tscr[1][:, 0:TOK], ALU.add, ALU.mult), "v")
                hrv[og] = c["v"]
                epd[og] = c["v"]
                if og >= 1:
                    emit_fold2(og - 1)
            emit_fold2(FT - 1)
            amax_bc(rowbF, rowbI, 0.5 / 127.0)
            g2v = {}
            for f in range(FT):
                p.drain("v")
                p.do("v", lambda h, ff=f: h.tensor_tensor(
                    tscr[0][:, 0:TOK], hrbuf_b[:, ff * TOK:(ff + 1) * TOK],
                    rowbS[:, 0:TOK], ALU.mult))
                p.drain("v")
                p.do("v", lambda h, ff=f: h.tensor_scalar(
                    hrbuf_b[:, ff * TOK:(ff + 1) * TOK], tscr[0][:, 0:TOK],
                    MAGIC, MAGIC, ALU.add, ALU.subtract), "v")
                g2v[f] = c["v"]

            dld, dmm, ep2, dstc = {}, {}, {}, {}
            qr = [0, 11, 22, 33, FT]

            def d_load(og):
                pg = og % 2
                if og < 2:
                    p.wait("y", "t", gumm[FT - 1])
                else:
                    p.wait("y", "t", dmm[og - 2])
                if og == 0:
                    p.wait("y", "cc", ccq_done["d"])
                for i in range(4):
                    s_ = "dw%d" % (pg * 4 + i)
                    for g in range(qr[i], qr[i + 1]):
                        p.dma("y", wbD[pg][:, g * 128:(g + 1) * 128],
                              dd4[d_slot[g], og], sem=s_)
                dld[og] = [("dw%d" % (pg * 4 + j), c["dw%d" % (pg * 4 + j)])
                           for j in range(4)]

            d_load(0)
            for og in range(KT):
                if og + 1 < KT:
                    d_load(og + 1)
                pg = og % 2
                for s_, v_ in dld[og]:
                    p.wait("t", s_, v_)
                if og >= 2:
                    p.wait("t", "v", ep2[og - 2])
                for g in range(FT):
                    if og == 0:
                        p.wait("t", "v", g2v[g])
                    for ch in range(2):
                        p.mm(pA[:, pg * 1024 + ch * 512:pg * 1024 + ch * 512 + 512],
                             wbD[pg][:, g * 128:(g + 1) * 128],
                             hrbuf_b[:, g * TOK + ch * 512:g * TOK + ch * 512 + 512],
                             start=(g == 0), stop=(g == FT - 1))
                dmm[og] = c["t"]
                xb = xtile[og % 2]
                dxs = "dx0" if og % 2 == 0 else "dx1"
                if og == 0:
                    p.wait("y", "v", c["v"])
                if og >= 2:
                    p.wait("y", "v", ep2[og - 2])
                    p.wait("y", "dst" if og % 2 == 0 else "d", dstc[og - 2])
                p.dma("y", xb[:], XS3[og], sem=dxs)
                ldx = c[dxs]
                p.wait("v", "t", dmm[og])
                p.wait("v", dxs, ldx)
                p.drain("v")
                p.do("v", lambda h, g_=pg: h.scalar_tensor_tensor(
                    tscr[0][:, 0:TOK], pA[:, g_ * 1024:(g_ + 1) * 1024], 0.0,
                    rowbF[:, 0:TOK], ALU.add, ALU.mult))
                p.drain("v")
                p.do("v", lambda h, b=xb: h.tensor_tensor(
                    b[:], b[:], tscr[0][:, 0:TOK], ALU.add), "v")
                ep2[og] = c["v"]
                p.wait("y", "v", ep2[og])
                sst = "dst" if og % 2 == 0 else "d"
                p.dma("y", XD3[og], xb[:], sem=sst)
                dstc[og] = c[sst]
            p.barrier()

        if "ffn1" in stages:
            ffn(X, X_SAVE)
        if "attn" not in stages:
            wait_all_dma("y")
            return

        ln_stats(X_SAVE)
        bcast(rows[0:1, 0:TOK], None, rowbM[:, 0:TOK])
        bc4 = bcast(rows[0:1, 1024:2048], None, rowbR[:, 0:TOK])
        p.wait("v", "dbc", bc4)
        p.wait("v", "dst", c["dst"])
        for k in range(KT):
            p.drain("v")
            p.do("v", lambda h, kk=k: h.scalar_tensor_tensor(
                tscr[0][:, 0:TOK], xres[:, kk * TOK:(kk + 1) * TOK], 0.0,
                rowbM[:, 0:TOK], ALU.add, ALU.subtract))
            p.drain("v")
            p.do("v", lambda h, kk=k: h.tensor_tensor(
                xh[:, kk * TOK:(kk + 1) * TOK], tscr[0][:, 0:TOK],
                rowbR[:, 0:TOK], ALU.mult), "v")
        xh_done = c["v"]
        p.wait("t", "v", xh_done)
        p.wait("g", "v", xh_done)

        for nh in range(2):
            for k in range(KT):
                p.mm(wA(nh, p1=16), wgtil[:, k * 16:(k + 1) * 16],
                     xh[:, k * TOK + nh * 512:k * TOK + (nh + 1) * 512],
                     start=(k == 0), stop=(k == KT - 1))
        bmm = c["t"]
        p.wait("s", "t", bmm)
        p.drain("s")
        for nh in range(2):
            p.do("s", lambda h, n=nh: h.activation(
                beta[:, n * 512:(n + 1) * 512], wA(n, p1=16), ACT.Sigmoid,
                bias=bgcol[:], scale=1.0), "s")
        sbeta = c["s"]
        p.wait("v", "s", sbeta)
        p.drain("v")
        p.do("v", lambda h: h.tensor_scalar(
            beta[:, 0:TOK], beta[:, 0:TOK], 0.8, 0.9995, ALU.max, ALU.min))
        p.do("v", lambda h: h.memset(z16[:, 0:128], 0.0))
        p.drain("v")
        for c_ in range(NCH):
            p.do("v", lambda h, cc=c_: h.tensor_tensor_scan(
                pbuf[:, cc * 128:(cc + 1) * 128], beta[:, cc * 128:(cc + 1) * 128],
                z16[:, 0:128], 1.0, ALU.mult, ALU.add), "v")
        pscan = c["v"]
        p.wait("t", "v", pscan)
        for c_ in range(NCH):
            p.tp(pA[:, 1024 + c_ * 16:1024 + (c_ + 1) * 16],
                 pbuf[:, c_ * 128:(c_ + 1) * 128], id16f[:])
        ptp = c["t"]
        p.wait("v", "t", ptp)
        p.drain("v")
        p.do("v", lambda h: h.tensor_copy(pcol[:], pA[:, 1024:1152]), "v")
        p.drain("v")
        recip_acc(rpcol[:], pcol[:], rpscr[:], "v")
        p.do("v", lambda h: h.sem_inc(p.sems["v"], 1))
        c["v"] += 1
        p.wait("g", "v", c["v"])
        p.do("g", lambda h: h.dma_start(
            out=prow[0:1, 0:128], in_=pcol[127:128, :]), "dqr", 16)
        vsl = c["dqr"]
        p.wait("t", "dqr", vsl)
        tm = p.mm(pA[:, 1536:1664], onesrow[:], prow[0:1, 0:128])
        p.wait("v", "t", tm)
        p.wait("v", "dqr", vsl)
        p.drain("v")
        p.do("v", lambda h: h.tensor_copy(slast[:], pA[:, 1536:1664]))
        p.do("v", lambda h: h.memset(state1[:, 0:H * 132], 0.0))
        p.drain("v")
        p.do("v", lambda h: h.memset(prow[0:1, 128:144], 1.0))
        p.drain("v")
        for c_ in range(1, NCH):
            p.do("v", lambda h, cc=c_: h.tensor_tensor(
                prow[0:1, 128 + cc * 16:144 + cc * 16],
                prow[0:1, 112 + cc * 16:128 + cc * 16],
                prow[0:1, cc * 16 - 16:cc * 16], ALU.mult))
            p.drain("v")
        p.do("v", lambda h: h.sem_inc(p.sems["v"], 1))
        c["v"] += 1
        initv = c["v"]
        p.wait("t", "v", initv)
        p.wait("g", "v", initv)

        wp_idx = [0]
        wp_tmm = {}

        def wproj_load(src_ap):
            i_ = wp_idx[0]
            buf = wproj_bufs[i_ % 2]
            if i_ >= 2:
                p.wait("g", "t", wp_tmm[i_ - 2])
            sems_ = []
            for q in range(4):
                s_ = "g2d%s" % ("abcdefgh"[(i_ % 2) * 4 + q])
                p.do("g", lambda h, q_=q: h.dma_start(
                    out=buf[:, q_ * 512:(q_ + 1) * 512].rearrange(
                        "p (k n) -> p k n", k=4),
                    in_=src_ap[:, q_ * 4:(q_ + 1) * 4, :]), s_, 16)
                sems_.append((s_, c[s_]))
            wp_idx[0] += 1
            return buf, sems_, i_

        def wproj_done(i_):
            wp_tmm[i_] = c["t"]

        pw_S = [pB[:, 0:128], pB[:, 512:640]]
        pw_Y = [pB[:, 1024:1153], pB[:, 1536:1665]]
        pw_T = [pA[:, 0:64].bitcast(BF16), pA[:, 1536:1600].bitcast(BF16)]
        pw_K = [pA[:, 512:641], pA[:, 1024:1153]]
        ascrP = [AR3b[:, 16384:16512], AR3b[:, 16512:16640]]
        ksbP = [AR3b[:, 16640:16768], AR3b[:, 16768:16896]]
        fscr = qfmf[:, 0:129]

        def rope_phi(src_psum, dst, sign_h):
            p.drain("v")
            p.do("v", lambda h: h.tensor_tensor(
                ropes[0][0:64, 0:TOK], src_psum[0:64, :], trigc[0:64, :],
                ALU.mult))
            p.do("v", lambda h: h.tensor_tensor(
                ropes[1][0:64, 0:TOK], src_psum[64:128, :], trigs[64:128, :],
                ALU.mult))
            p.do("v", lambda h: h.tensor_tensor(
                ropes[0][64:128, 0:TOK], src_psum[0:64, :], trigs[0:64, :],
                ALU.mult))
            p.do("v", lambda h: h.tensor_tensor(
                ropes[1][64:128, 0:TOK], src_psum[64:128, :], trigc[64:128, :],
                ALU.mult), "v")
            pread = c["v"]
            p.drain("v")
            p.do("v", lambda h: h.tensor_tensor(
                kropef[0:64, 0:TOK], ropes[0][0:64, 0:TOK],
                ropes[1][0:64, 0:TOK], ALU.subtract))
            p.do("v", lambda h: h.tensor_tensor(
                kropef[64:128, 0:TOK], ropes[0][64:128, 0:TOK],
                ropes[1][64:128, 0:TOK], ALU.add))
            p.drain("v")
            p.do("v", lambda h: h.tensor_scalar(
                ropes[0][:, 0:TOK], kropef[:, 0:TOK], 0.0, None, ALU.min), "v")
            mn = c["v"]
            p.wait("s", "v", mn)
            p.drain("s")
            ex = p.do("s", lambda h: h.activation(
                ropes[1][:, 0:TOK], ropes[0][:, 0:TOK], ACT.Exp), "s")
            p.wait("v", "s", ex)
            p.drain("v")
            p.do("v", lambda h: h.scalar_tensor_tensor(
                dst, kropef[:, 0:TOK], 0.0, ropes[1][:, 0:TOK],
                ALU.max, ALU.add), "v")
            return pread, c["v"]

        hend_v = {}
        for h_ in range(H):
            wprojQ, ldq, wpiQ = wproj_load(WT["wq"][h_])
            for s_, v_ in ldq:
                p.wait("t", s_, v_)
            p.wait("t", "v", hend_v.get(h_ - 1, initv))
            for nh in range(2):
                for k in range(KT):
                    p.mm(pA[:, nh * 512:(nh + 1) * 512],
                         wprojQ[:, k * 128:(k + 1) * 128],
                         xh[:, k * TOK + nh * 512:k * TOK + (nh + 1) * 512],
                         start=(k == 0), stop=(k == KT - 1))
            qmm = c["t"]
            wproj_done(wpiQ)
            wprojK, ldk, wpiK = wproj_load(WT["wk"][h_])
            for s_, v_ in ldk:
                p.wait("t", s_, v_)
            for nh in range(2):
                for k in range(KT):
                    p.mm(pA[:, 1024 + nh * 512:1024 + (nh + 1) * 512],
                         wprojK[:, k * 128:(k + 1) * 128],
                         xh[:, k * TOK + nh * 512:k * TOK + (nh + 1) * 512],
                         start=(k == 0), stop=(k == KT - 1))
            kmm = c["t"]
            wproj_done(wpiK)
            wprojV, ldv, wpiV = wproj_load(WT["wv"][h_])
            for s_, v_ in ldv:
                p.wait("t", s_, v_)
            for nh in range(2):
                for k in range(KT):
                    p.mm(pB[:, nh * 512:(nh + 1) * 512],
                         wprojV[:, k * 128:(k + 1) * 128],
                         xh[:, k * TOK + nh * 512:k * TOK + (nh + 1) * 512],
                         start=(k == 0), stop=(k == KT - 1))
            vmm = c["t"]
            wproj_done(wpiV)

            p.wait("v", "t", qmm)
            _, qphi_v = rope_phi(pA[:, 0:1024], qphi[:, h_ * TOK:(h_ + 1) * TOK],
                                 h_)
            p.wait("v", "t", kmm)
            _, kphi_v = rope_phi(pA[:, 1024:2048], kfm[:, 0:TOK], h_)
            if "p1a" in stages and h_ == 0:
                wait_all_dma("y")
                return
            vfm_bf = ropes[0][:].bitcast(BF16)
            vts = kropef[:].bitcast(BF16)
            p.wait("v", "t", vmm)
            p.drain("v")
            p.do("v", lambda h: h.tensor_copy(vfm_bf[:, 0:TOK], pB[:, 0:TOK]))
            p.drain("v")
            p.do("v", lambda h: h.memset(
                vts[:, 0:NCH * 132].rearrange("p (n e) -> p n e", e=132)[:, :, 128:129],
                1.0), "v")
            vts_init = c["v"]
            p.wait("t", "v", vts_init)
            vcp = {}
            for c_ in range(NCH):
                if c_ >= 2:
                    p.wait("t", "v", vcp[c_ - 2])
                ttp = p.tp(pw_T[c_ % 2], vfm_bf[:, c_ * 128:(c_ + 1) * 128],
                           id128b[:])
                p.wait("v", "t", ttp)
                p.drain("v")
                p.do("v", lambda h, cc=c_: h.tensor_copy(
                    vts[:, cc * 132:cc * 132 + 128], pw_T[cc % 2]), "v")
                vcp[c_] = c["v"]
            vts_done = c["v"]
            if "p1b" in stages and h_ == 0:
                wait_all_dma("y")
                return

            mask_v, ks_v, st_v, kvt, ymt = {}, {}, {}, {}, {}
            nch_h = 0 if ("noc%d" % h_) in stages else NCH
            for c_ in range(nch_h + 1 if nch_h else 0):
                if c_ < nch_h:
                    pw = c_ % 2
                    ii = idx(c_, h_)
                    if c_ == 0:
                        p.wait("t", "v", max(qphi_v, kphi_v, vts_done))
                    if "nos" not in stages:
                        if c_ >= 2:
                            p.wait("t", "v", mask_v[c_ - 2])
                        smm = p.mm(pw_S[pw], kfm[:, c_ * 128:(c_ + 1) * 128],
                                   qphi[:, h_ * TOK + c_ * 128:h_ * TOK + (c_ + 1) * 128])
                        p.wait("v", "t", smm)
                        p.drain("v")
                        p.do("v", lambda h, i_=ii, w=pw: h.scalar_tensor_tensor(
                            ascrP[w][:], pw_S[w], rpcol[:, i_:i_ + 1], maskt[:],
                            ALU.mult, ALU.mult), "v")
                        mask_v[c_] = c["v"]
                    else:
                        mask_v[c_] = c["v"]
                    if "nok" not in stages:
                        if c_ >= 2:
                            p.wait("t", "v", ks_v[c_ - 2])
                        ktp = p.tp(pw_T[pw], kfm[:, c_ * 128:(c_ + 1) * 128],
                                   id128b[:])
                        p.wait("v", "t", ktp)
                        p.drain("v")
                        p.do("v", lambda h, i_=ii, w=pw: h.tensor_scalar(
                            ksbP[w][:], pw_T[w], rpcol[:, i_:i_ + 1], None,
                            ALU.mult), "v")
                        ks_v[c_] = c["v"]
                    else:
                        ks_v[c_] = c["v"]
                if c_ >= 1:
                    cc = c_ - 1
                    pw = cc % 2
                    ii = idx(cc, h_)
                    if "nokv" not in stages:
                        p.wait("t", "v", ks_v[cc])
                        if cc >= 2:
                            p.wait("t", "v", st_v[cc - 2])
                        kvmm = p.mm(pw_K[pw], ksbP[pw][:],
                                    vts[:, cc * 132:cc * 132 + 129])
                        kvt[cc] = c["t"]
                    if "noy" not in stages:
                        if cc >= 1:
                            p.wait("t", "v", st_v[cc - 1])
                        p.mm(pw_Y[pw], ascrP[pw][:],
                             vts[:, cc * 132:cc * 132 + 129],
                             start=True, stop=False)
                        p.mm(pw_Y[pw],
                             qphi[:, h_ * TOK + cc * 128:h_ * TOK + (cc + 1) * 128],
                             st_sl(state1, h_), start=False, stop=True)
                        ymt[cc] = c["t"]
                    if "nokv" not in stages:
                        p.wait("v", "t", kvt[cc])
                        p.drain("v")
                        p.do("v", lambda h, hh=h_, ci=cc: h.tensor_copy(
                            kv_sl(hh, ci), pw_K[ci % 2]))
                        p.drain("v")
                        p.do("v", lambda h, hh=h_, ci=cc: h.tensor_tensor(
                            fscr, st_sl(state1, hh), kv_sl(hh, ci), ALU.add), "v")
                        if "noy" not in stages:
                            p.wait("v", "t", ymt[cc])
                        p.drain("v")
                        p.do("v", lambda h, hh=h_, i_=ii: h.tensor_scalar(
                            st_sl(state1, hh), fscr, slast[:, i_:i_ + 1], None,
                            ALU.mult), "v")
                        p.drain("v")
                    if "noy" not in stages:
                        p.wait("v", "t", ymt[cc])
                        p.drain("v")
                        p.do("v", lambda h, hh=h_, ci=cc: h.tensor_copy(
                            yint_sl(hh, ci), pw_Y[ci % 2]), "v")
                    st_v[cc] = c["v"]
            hend_v[h_] = c["v"]
            if "p1h%d" % h_ in stages:
                wait_all_dma("y")
                return
        if "p1" in stages:
            wait_all_dma("y")
            return

        p.drain("v")
        p.do("v", lambda h: h.sem_inc(p.sems["bv"], 1))
        c["bv"] += 1
        p.wait("y", "bv", c["bv"])
        p.dma("y", CC_IN.rearrange("(hh pp) e -> pp hh e", pp=128),
              state1[:, 0:H * 132].rearrange("p (hh e) -> p hh e", e=132),
              sem="dcc")
        ccin_d = c["dcc"]
        p.wait("g", "dcc", ccin_d)
        p.do("g", lambda h: h.collective_compute(
            "AllGather", ALU.bypass, ins=[CC_IN], outs=[CC_OUT],
            replica_groups=groups), "cc", 1)
        p.wait("y", "cc", c["cc"])
        p.dma("y", staterecv[:, 0:H * 132].rearrange("p (hh e) -> p hh e", e=132),
              CC_OUT[0:H * 128, :].rearrange("(hh pp) e -> pp hh e", pp=128),
              sem="dcc")
        ccout_d = c["dcc"]
        p.wait("v", "dcc", ccout_d)
        p.drain("v")
        p.do("v", lambda h: h.tensor_scalar(
            state2[:, 0:H * 132], staterecv[:, 0:H * 132], mrecv[:], None,
            ALU.mult), "v")
        s2init = c["v"]
        p.wait("t", "v", s2init)
        if "cc1" in stages:
            wait_all_dma("y")
            return

        yfm = xh
        picol = ropes[0]
        tmpi = p.mm(pA[:, 1536:1664], onesrow[:], prow[0:1, 128:256])
        p.wait("v", "t", tmpi)
        p.drain("v")
        p.do("v", lambda h: h.tensor_copy(picol[:, 0:128], pA[:, 1536:1664]), "v")
        pic_v = c["v"]
        p.wait("t", "v", pic_v)
        uP = [qfmf[:, 256:388], qfmf[:, 388:520]]
        ytmpP = ksbP
        pw_tp = [pB_tpy, pB_tpv]
        order2 = [(hh, cc) for hh in range(H) for cc in range(NCH)]
        immt, ytp_t, yn_v = {}, {}, {}
        for j, (h_, c_) in enumerate(order2):
            ii = idx(c_, h_)
            pw = j % 2
            if j >= 2:
                p.wait("t", "v", yn_v[j - 2])
            p.mm(pA[:, pw * 512:pw * 512 + 129],
                 qphi[:, h_ * TOK + c_ * 128:h_ * TOK + (c_ + 1) * 128],
                 st_sl(state2, h_))
            immt[j] = c["t"]
            if j >= 1:
                p.wait("t", "v", yn_v[j - 1])
                ytp_t[j - 1] = p.tp(pw_tp[1 - pw], ytmpP[1 - pw][:], id128b[:])
            p.wait("v", "t", immt[j])
            p.drain("v")
            p.do("v", lambda h, w=pw, i_=ii, hh=h_, ci=c_: h.scalar_tensor_tensor(
                uP[w][:, 0:129], pA[:, w * 512:w * 512 + 129],
                picol[:, i_:i_ + 1], yint_sl(hh, ci), ALU.mult, ALU.add))
            p.drain("v")
            p.do("v", lambda h, w=pw, i_=ii: h.tensor_scalar(
                denc[:, w * 4:w * 4 + 1], uP[w][:, 128:129],
                pcol[:, i_:i_ + 1], 1e-6, ALU.mult, ALU.add))
            p.drain("v")
            p.do("v", lambda h, w=pw: h.reciprocal(
                denc[:, w * 4 + 1:w * 4 + 2], denc[:, w * 4:w * 4 + 1]))
            p.drain("v")
            p.do("v", lambda h, w=pw, i_=ii: h.tensor_tensor(
                denc[:, w * 4 + 2:w * 4 + 3], denc[:, w * 4 + 1:w * 4 + 2],
                pcol[:, i_:i_ + 1], ALU.mult))
            if j >= 2:
                p.wait("v", "t", ytp_t[j - 2])
            p.drain("v")
            p.do("v", lambda h, w=pw: h.tensor_scalar(
                ytmpP[w][:], uP[w][:, 0:128], denc[:, w * 4 + 2:w * 4 + 3],
                None, ALU.mult), "v")
            yn_v[j] = c["v"]
            if j >= 1:
                jp = j - 1
                hp, cp = order2[jp]
                p.wait("v", "t", ytp_t[jp])
                p.drain("v")
                p.do("v", lambda h, hh=hp, ci=cp, w=1 - pw: h.tensor_copy(
                    yfm[:, hh * TOK + ci * 128:hh * TOK + (ci + 1) * 128],
                    pw_tp[w]), "v")
        jl = len(order2) - 1
        hp, cp = order2[jl]
        p.wait("t", "v", c["v"])
        ytp_t[jl] = p.tp(pw_tp[jl % 2], ytmpP[jl % 2][:], id128b[:])
        p.wait("v", "t", ytp_t[jl])
        p.drain("v")
        p.do("v", lambda h, hh=hp, ci=cp: h.tensor_copy(
            yfm[:, hh * TOK + ci * 128:hh * TOK + (ci + 1) * 128],
            pw_tp[jl % 2]), "v")
        yfm_done = c["v"]
        p.wait("t", "v", yfm_done)
        p.wait("g", "v", yfm_done)
        if "p2" in stages:
            wait_all_dma("y")
            return

        wait_all_dma("y")
        for ot in range(KT):
            wproj, ldw, wpi = wproj_load(WT["wo"][ot])
            for s_, v_ in ldw:
                p.wait("t", s_, v_)
            p.wait("t", "v", c["v"])
            for nh in range(2):
                for k in range(KT):
                    p.mm(wA(nh), wproj[:, k * 128:(k + 1) * 128],
                         yfm[:, k * TOK + nh * 512:k * TOK + (nh + 1) * 512],
                         start=(k == 0), stop=(k == KT - 1))
            omm = c["t"]
            wproj_done(wpi)
            xb = xtile[ot % 2]
            dxs = "dx0" if ot % 2 == 0 else "dx1"
            if ot == 0:
                p.wait("y", "v", c["v"])
            if ot >= 2:
                p.wait("y", "v", c["v"])
            p.dma("y", xb[:], X_SAVE.rearrange("(k p) t -> k p t", p=128)[ot],
                  sem=dxs)
            ldx = c[dxs]
            p.wait("v", "t", omm)
            p.wait("v", dxs, ldx)
            p.wait("v", "dst", c["dst"])
            p.drain("v")
            p.do("v", lambda h, o=ot, b=xb: h.scalar_tensor_tensor(
                tscr[0][:, 0:TOK], pA[:, 0:1024],
                bocol[:, o:o + 1], b[:], ALU.add, ALU.add), "v")
            wo_v = c["v"]
            p.wait("y", "v", wo_v)
            p.dma("y", X_SAVE.rearrange("(k p) t -> k p t", p=128)[ot],
                  tscr[0][:, 0:TOK], sem="dst")
        p.barrier()

        if "ffn2" not in stages:
            wait_all_dma("y")
            return
        ffn(X_SAVE, OUT)

        wait_all_dma("y")

    with nc.Block() as block:
        @block.tensor
        def _(tensor):
            script(P("t", tensor, sems))

        @block.vector
        def _(vector):
            script(P("v", vector, sems))

        @block.scalar
        def _(scalar):
            script(P("s", scalar, sems))

        @block.gpsimd
        def _(gpsimd):
            script(P("g", gpsimd, sems))

        @block.sync
        def _(sync):
            script(P("y", sync, sems))

    es.close()
    return nc


def _host_inputs(inputs, n_cores=8):
    x = np.ascontiguousarray(np.asarray(inputs["L0_attn_x"], dtype=np.float32))

    def tiled(w):
        w = np.asarray(w, np.float32)
        t = w.reshape(KT, 128, KT, 128).transpose(2, 1, 0, 3)
        return np.ascontiguousarray(t)

    wq_t = tiled(inputs["L0_attn_Wq"])
    wk_t = tiled(inputs["L0_attn_Wk"])
    wv_t = tiled(inputs["L0_attn_Wv"])
    wo_t = tiled(inputs["L0_attn_Wo"])
    wg = np.ascontiguousarray(np.asarray(inputs["L0_attn_Wg"], np.float32))
    bg = np.ascontiguousarray(np.asarray(inputs["L0_attn_bg"], np.float32).reshape(H, 1))
    bo = np.ascontiguousarray(
        np.asarray(inputs["L0_attn_bo"], np.float32).reshape(KT, 128).T)
    wgt = np.asarray(inputs["L0_ffn_w_gate"], np.float32).T
    wut = np.asarray(inputs["L0_ffn_w_up"], np.float32).T
    wdt = np.asarray(inputs["L0_ffn_w_down"], np.float32).T
    qg = KT // n_cores
    qd = -(-FT // n_cores)
    d_counts = [FT // n_cores + (1 if i < FT % n_cores else 0)
                for i in range(n_cores)]
    d_start = [sum(d_counts[:i]) for i in range(n_cores)]

    maskt = (np.arange(128)[:, None] <= np.arange(128)[None, :]).astype(np.float32)
    import ml_dtypes
    id128b = np.eye(128, dtype=ml_dtypes.bfloat16)
    id128f = np.eye(128, dtype=np.float32)
    id16f = np.eye(16, dtype=np.float32)
    onescol = np.ones((128, 1), np.float32)
    ones128 = np.ones((128, 128), np.float32)
    constt = np.array([[1e-5, 1.0 / 128.0, 0.0, 0.0]], np.float32)
    onesrow = np.ones((1, 128), np.float32)

    half = 64
    inv = 1.0 / (10000.0 ** (np.arange(half, dtype=np.float64) / half))

    maps = []
    for core in range(n_cores):
        b, hf = core // 2, core % 2
        t0 = hf * TOK
        pos = np.arange(t0, t0 + TOK, dtype=np.float64)
        fr = np.outer(inv, pos)
        cosv = np.cos(fr).astype(np.float32)
        sinv = np.sin(fr).astype(np.float32)
        trigc = np.ascontiguousarray(np.concatenate([cosv, cosv], 0))
        trigs = np.ascontiguousarray(np.concatenate([sinv, sinv], 0))
        x_fm = np.ascontiguousarray(x[b, t0:t0 + TOK, :].T)
        mr = np.full((128, 1), float(hf), np.float32)
        wown_g = np.ascontiguousarray(
            wgt[core * qg * 128:(core + 1) * qg * 128])
        wown_u = np.ascontiguousarray(
            wut[core * qg * 128:(core + 1) * qg * 128])
        wown_d = np.zeros((qd * 128, D), np.float32)
        nd = d_counts[core]
        wown_d[:nd * 128] = wdt[d_start[core] * 128:(d_start[core] + nd) * 128]
        maps.append(dict(
            x_fm=x_fm, wq=wq_t, wk=wk_t, wv=wv_t, wo=wo_t, wg=wg, bg=bg, bo=bo,
            wown_g=wown_g, wown_u=wown_u, wown_d=wown_d,
            trigc=trigc, trigs=trigs, maskt=maskt,
            id128b=id128b, id128f=id128f, id16f=id16f, onescol=onescol,
            ones128=ones128, onesrow=onesrow, constt=constt,
            mrecv=mr))
    return maps


def kernel(**inputs):
    n_cores = 8
    maps = _host_inputs(inputs, n_cores)
    nc = build_program(n_cores)
    res = run_bass_kernel_spmd(nc, maps, core_ids=list(range(n_cores)))
    x = np.asarray(inputs["L0_attn_x"])
    B, L, _ = x.shape
    out = np.empty((B, L, D), np.float32)
    for core in range(n_cores):
        b, hf = core // 2, core % 2
        out[b, hf * TOK:(hf + 1) * TOK, :] = np.asarray(
            res.results[core]["out_fm"]).T
    return out
